# version 06
# baseline (speedup 1.0000x reference)
import numpy as np
import ml_dtypes
from contextlib import ExitStack

import concourse.bass as bass
import concourse.mybir as mybir
from concourse.bass_utils import run_bass_kernel_spmd

F32 = mybir.dt.float32
BF16 = mybir.dt.bfloat16
AF = mybir.ActivationFunctionType
ALU = mybir.AluOpType
AX = mybir.AxisListType.X

D = 1024
H = 16
DH = 64
LAT = 256
NE = 16
HID = 1024
SCALE = DH ** -0.5
EPS = 1e-5


class Ctr:
    def __init__(self, nc, es, name):
        self.h = es.enter_context(nc.semaphore(name))
        self.v = 0

    def bump(self, n=1):
        self.v += n
        return self.v


def build_nc(S=4096, debug=False):
    nT = S // 128
    nQB = S // 512
    nJB = S // 128
    nG = S // 512
    nTTH = nT // 2

    nc = bass.Bass("TRN2", target_bir_lowering=False)

    x_d = nc.dram_tensor("x", [S, D], F32, kind="ExternalInput").ap()
    Wq_d = nc.dram_tensor("Wq", [D, D], F32, kind="ExternalInput").ap()
    Wl_d = nc.dram_tensor("Wl", [D, LAT], F32, kind="ExternalInput").ap()
    Wk_d = nc.dram_tensor("Wk", [LAT, D], F32, kind="ExternalInput").ap()
    Wv_d = nc.dram_tensor("Wv", [LAT, D], F32, kind="ExternalInput").ap()
    Wo_d = nc.dram_tensor("Wo", [D, D], F32, kind="ExternalInput").ap()
    Wr_d = nc.dram_tensor("Wr", [D, NE], F32, kind="ExternalInput").ap()
    bqT_d = nc.dram_tensor("bqT", [128, 8], F32, kind="ExternalInput").ap()
    bkT_d = nc.dram_tensor("bkT", [128, 8], F32, kind="ExternalInput").ap()
    blT_d = nc.dram_tensor("blT", [128, 2], F32, kind="ExternalInput").ap()
    bv_d = nc.dram_tensor("bv", [1, D], F32, kind="ExternalInput").ap()
    bo_d = nc.dram_tensor("bo", [1, D], F32, kind="ExternalInput").ap()
    br_d = nc.dram_tensor("br", [1, NE], F32, kind="ExternalInput").ap()
    W1_d = nc.dram_tensor("W1", [NE, D, HID], F32, kind="ExternalInput").ap()
    b1_d = nc.dram_tensor("b1", [NE, HID], F32, kind="ExternalInput").ap()
    g1T_d = nc.dram_tensor("g1T", [128, 128], F32, kind="ExternalInput").ap()
    be1T_d = nc.dram_tensor("be1T", [128, 128], F32, kind="ExternalInput").ap()
    W2_d = nc.dram_tensor("W2", [NE, HID, D], F32, kind="ExternalInput").ap()
    b2_d = nc.dram_tensor("b2", [NE, D], F32, kind="ExternalInput").ap()
    eye_d = nc.dram_tensor("eye", [128, 128], F32, kind="ExternalInput").ap()
    out_d = nc.dram_tensor("out", [S, D], F32, kind="ExternalOutput").ap()
    ikind = "ExternalOutput" if debug else "Internal"
    xnT_hbm = nc.dram_tensor("xnT_hbm", [128, 8 * S], BF16, kind=ikind).ap()
    acc_hbm = nc.dram_tensor("acc_hbm", [S, D], F32, kind=ikind).ap()
    gdbg = nc.dram_tensor("gdbg", [128, NE * (S // 128)], F32,
                          kind="ExternalOutput").ap() if debug else None
    if debug:
        kdbg = nc.dram_tensor("kdbg", [128, 8 * S], BF16, kind="ExternalOutput").ap()
        qdbg = nc.dram_tensor("qdbg", [128, 2 * (S // 512) * 2048], BF16,
                              kind="ExternalOutput").ap()
        vdbg = nc.dram_tensor("vdbg", [128, 2 * (S // 128) * 520], BF16,
                              kind="ExternalOutput").ap()
        aodbg = nc.dram_tensor("aodbg", [128, 2 * (S // 512) * 2048], BF16,
                               kind="ExternalOutput").ap()
        mdbg = nc.dram_tensor("mdbg", [128, 2 * (S // 512) * 8], F32,
                              kind="ExternalOutput").ap()

    xnTh3 = xnT_hbm.rearrange("p (c s) -> p c s", c=8)
    acc3 = acc_hbm.rearrange("(q p) d -> p q d", p=128)
    out3 = out_d.rearrange("(q p) d -> p q d", p=128)

    ENGS = ("sync", "gpsimd", "vector", "scalar", "tensor")
    sched = {e: [] for e in ENGS}

    def em(eng, fn):
        sched[eng].append(fn)

    def flush_phase():
        with nc.Block() as blk:
            @blk.sync
            def _(e):
                for f in sched["sync"]:
                    f(e)

            @blk.gpsimd
            def _(e):
                for f in sched["gpsimd"]:
                    f(e)

            @blk.vector
            def _(e):
                for f in sched["vector"]:
                    f(e)

            @blk.scalar
            def _(e):
                for f in sched["scalar"]:
                    f(e)

            @blk.tensor
            def _(e):
                for f in sched["tensor"]:
                    f(e)
        for e in ENGS:
            sched[e].clear()

    es_top = ExitStack()

    def sbuf(name, shape, dt, stack):
        return stack.enter_context(nc.sbuf_tensor(name, shape, dt))

    sp = Ctr(nc, es_top, "sp")
    sv = Ctr(nc, es_top, "sv")
    sa = Ctr(nc, es_top, "sa")
    s_wA = Ctr(nc, es_top, "s_wA")
    s_bias = Ctr(nc, es_top, "s_bias")
    s_xt = [Ctr(nc, es_top, f"s_xt{i}") for i in range(2)]
    s_spl = [Ctr(nc, es_top, f"s_spl{i}") for i in range(2)]
    s_wg = Ctr(nc, es_top, "s_wg")
    s_qb = [Ctr(nc, es_top, f"s_qb{i}") for i in range(2)]
    s_x2 = [Ctr(nc, es_top, f"s_x2{i}") for i in range(2)]
    s_ac = [Ctr(nc, es_top, f"s_ac{i}") for i in range(2)]
    s_we = [Ctr(nc, es_top, f"s_we{i}") for i in range(2)]
    s_w2 = [Ctr(nc, es_top, f"s_w2{i}") for i in range(2)]
    s_xh = Ctr(nc, es_top, "s_xh")
    s_yi = Ctr(nc, es_top, "s_yi")
    s_out = Ctr(nc, es_top, "s_out")
    s_eye = Ctr(nc, es_top, "s_eye")
    s_dbg = Ctr(nc, es_top, "s_dbg") if debug else None

    def wait(eng, ctr, val):
        if val > 0:
            em(eng, lambda e, c=ctr.h, v=val: e.wait_ge(c, v))

    def mm(out_ap, lhsT, rhs, start, stop, inc=False, tp=None):
        if inc:
            val = sp.bump()
            em("tensor", lambda e, o=out_ap, l=lhsT, r=rhs, s1=start, s2=stop, t=tp:
               e.matmul(o, l, r, start=s1, stop=s2, tile_position=t).then_inc(sp.h, 1))
            return val
        em("tensor", lambda e, o=out_ap, l=lhsT, r=rhs, s1=start, s2=stop, t=tp:
           e.matmul(o, l, r, start=s1, stop=s2, tile_position=t))
        return None

    def act(out_ap, in_ap, func, bias=0.0, scale=1.0, inc=False):
        if inc:
            val = sa.bump()
            em("scalar", lambda e, o=out_ap, i=in_ap, f=func, b=bias, s=scale:
               e.activation(o, i, f, bias=b, scale=s).then_inc(sa.h, 1))
            return val
        em("scalar", lambda e, o=out_ap, i=in_ap, f=func, b=bias, s=scale:
           e.activation(o, i, f, bias=b, scale=s))
        return None

    def vop(name, *args, inc=False, **kw):
        if inc:
            val = sv.bump()
            em("vector", lambda e, n=name, a=args, k=kw:
               getattr(e, n)(*a, **k).then_inc(sv.h, 1))
            return val
        em("vector", lambda e, n=name, a=args, k=kw: getattr(e, n)(*a, **k))
        return None

    def dma(eng, ctr, out_ap, in_ap):
        val = ctr.bump(16)
        em(eng, lambda e, o=out_ap, i=in_ap, c=ctr.h:
           e.dma_start(o, i).then_inc(c, 16))
        return val

    ones1 = sbuf("ones1", [1, 128], BF16, es_top)
    ones64 = sbuf("ones64", [1, 64], BF16, es_top)
    ind_lo = sbuf("ind_lo", [128, 1], BF16, es_top)
    ind_hi = sbuf("ind_hi", [128, 1], BF16, es_top)
    I128 = sbuf("I128", [128, 128], BF16, es_top)
    epsT = sbuf("epsT", [128, 1], F32, es_top)
    lnS = sbuf("lnS", [1, 1], F32, es_top)
    g1T = sbuf("g1Ts", [128, 128], F32, es_top)
    be1T = sbuf("be1Ts", [128, 128], F32, es_top)
    gates_all = sbuf("gates_all", [128, NE * nT], F32, es_top)

    PA = es_top.enter_context(nc.psum_tensor("PA", [128, 1024], F32))
    PB = es_top.enter_context(nc.psum_tensor("PB", [128, 1024], F32))
    PC = es_top.enter_context(nc.psum_tensor("PC", [128, 512], F32))
    PD = es_top.enter_context(nc.psum_tensor("PD", [128, 512], F32))
    PE_ = es_top.enter_context(nc.psum_tensor("PE", [128, 512], F32))
    PF = es_top.enter_context(nc.psum_tensor("PF", [128, 512], F32))
    PA3 = PA[:, :].rearrange("p (c t) -> p c t", c=8)
    PB3 = PB[:, :].rearrange("p (c t) -> p c t", c=8)

    dma("gpsimd", s_eye, I128[:, :], eye_d)
    for dst, src in ((g1T[:, :], g1T_d), (be1T[:, :], be1T_d)):
        dma("sync", s_bias, dst, src)
    vop("memset", ones1[:, :], 1.0)
    vop("memset", ones64[:, :], 1.0)
    vop("memset", ind_lo[0:64, :], 1.0)
    vop("memset", ind_lo[64:128, :], 0.0)
    vop("memset", ind_hi[0:64, :], 0.0)
    vop("memset", ind_hi[64:128, :], 1.0)
    vop("memset", epsT[:, :], EPS)
    consts_done = vop("memset", lnS[:, :], float(np.log(SCALE)), inc=True)

    es_attn = ExitStack()
    bqT = sbuf("bqTs", [128, 8], F32, es_attn)
    bkT = sbuf("bkTs", [128, 8], F32, es_attn)
    blT = sbuf("blTs", [128, 2], F32, es_attn)
    bv_bf = sbuf("bv_bf", [1, D], BF16, es_attn)
    bo_bf = sbuf("bo_bf", [1, D], BF16, es_attn)
    br_bf = sbuf("br_bf", [1, NE], BF16, es_attn)
    b2pad = sbuf("b2pad", [128, D], BF16, es_attn)
    gatesT_pad = sbuf("gatesT_pad", [128, S], BF16, es_attn)
    Wr_s = sbuf("Wr_s", [128, 8 * NE], BF16, es_attn)
    latT = sbuf("latT", [128, 2 * S], BF16, es_attn)
    for dst, srcp in (
        (bv_bf[0:1, :], bv_d),
        (bo_bf[0:1, :], bo_d),
        (br_bf[0:1, :], br_d),
        (Wr_s[:, :].rearrange("p (c m) -> p c m", c=8),
         Wr_d.rearrange("(c p) m -> p c m", p=128)),
    ):
        dma("gpsimd", s_wA, dst, srcp)
    for dst, srcp in ((bqT[:, :], bqT_d), (bkT[:, :], bkT_d), (blT[:, :], blT_d)):
        dma("sync", s_bias, dst, srcp)
    vop("memset", gatesT_pad[:, :], 0.0)
    consts2_done = vop("memset", b2pad[:, :], 0.0, inc=True)
    wait("gpsimd", sv, consts2_done)
    dma("gpsimd", s_wA, b2pad[0:16, :], b2_d)
    kT_g = sbuf("kT_g", [128, 4 * S], BF16, es_attn)
    v_aug = sbuf("v_aug", [128, nJB * 520], BF16, es_attn)
    Wl_s = sbuf("Wl_s", [128, 8 * LAT], BF16, es_attn)
    Wq_g = sbuf("Wq_g", [128, 8 * 512], BF16, es_attn)
    Wo_g = sbuf("Wo_g", [128, 4 * 1024], BF16, es_attn)
    Wk_g = sbuf("Wk_g", [128, 2 * 512], BF16, es_attn)
    Wv_g = sbuf("Wv_g", [128, 2 * 512], BF16, es_attn)
    xnT_qb = sbuf("xnT_qb", [128, 8 * 512], BF16, es_attn)
    qT_qb = sbuf("qT_qb", [128, 4 * 512], BF16, es_attn)
    pT = [[sbuf(f"pT{h}_{i}", [128, 512], BF16, es_attn) for i in range(2)]
          for h in range(2)]
    attn_oT = sbuf("attn_oT", [128, 4 * 512], BF16, es_attn)
    sqs = sbuf("sqs", [128, 512], BF16, es_attn)
    krow8a = sbuf("krow8a", [1, 8], F32, es_attn)
    krow8b = sbuf("krow8b", [1, 8], F32, es_attn)
    kn1 = sbuf("kn1", [1, 8], F32, es_attn)
    qn1 = sbuf("qn1", [1, 8], F32, es_attn)
    t8 = sbuf("t8", [1, 8], F32, es_attn)
    l8 = sbuf("l8", [1, 8], F32, es_attn)
    m8 = sbuf("m8", [1, 8], F32, es_attn)
    negm8 = sbuf("negm8", [1, 8], F32, es_attn)
    negm8b = sbuf("negm8b", [1, 8], BF16, es_attn)
    Mrep = sbuf("Mrep", [128, 8], F32, es_attn)
    rec1 = sbuf("rec1", [1, 512], F32, es_attn)
    rec1b = sbuf("rec1b", [1, 512], BF16, es_attn)
    recb = sbuf("recb", [64, 512], F32, es_attn)
    x2 = [sbuf(f"x2_{i}", [128, D], F32, es_attn) for i in range(2)]
    acc_st = [sbuf(f"acc_st{i}", [128, D], F32, es_attn) for i in range(2)]
    v_aug3 = v_aug[:, :].rearrange("p (j h dd) -> p j h dd", h=8, dd=65)

    es_p1 = ExitStack()
    xt = [sbuf(f"xt{i}", [128, D], F32, es_p1) for i in range(2)]
    xnbf = [sbuf(f"xnbf{i}", [128, D], BF16, es_p1) for i in range(2)]
    xnW = [sbuf(f"xnW{i}", [128, 8 * 512], BF16, es_p1) for i in range(2)]
    st6 = sbuf("st6", [128, 12], F32, es_p1)
    mvA = sbuf("mvA", [128, 2], F32, es_p1)
    mvB = sbuf("mvB", [128, 2], F32, es_p1)
    msvs = sbuf("msvs", [128, 2], F32, es_p1)
    dmt = sbuf("dmt", [128, 2], F32, es_p1)
    var4 = sbuf("var4", [128, 1], F32, es_p1)
    t1 = sbuf("t1", [128, 1], F32, es_p1)
    rstd = sbuf("rstd", [128, 1], F32, es_p1)
    nmr = sbuf("nmr", [128, 1], F32, es_p1)
    negm1 = sbuf("negm1", [128, 1], F32, es_p1)
    exps = sbuf("exps", [128, NE], F32, es_p1)
    mx8 = sbuf("mx8", [128, 8], F32, es_p1)
    den = sbuf("den", [128, 1], F32, es_p1)
    rden = sbuf("rden", [128, 1], F32, es_p1)
    maskg = sbuf("maskg", [128, NE], F32, es_p1)
    gp_t = sbuf("gp_t", [128, NE], F32, es_p1)
    gbf = sbuf("gbf", [128, NE], BF16, es_p1)
    xnW3 = [w[:, :].rearrange("p (c t) -> p c t", c=8) for w in xnW]

    dma("gpsimd", s_wA, Wl_s[:, :].rearrange("p (c m) -> p c m", c=8),
        Wl_d.rearrange("(c p) m -> p c m", p=128))
    wA_all = s_wA.v

    xt_free = [0, 0]
    tr_bank_free = [0, 0]
    ps6_free_a, ps6_free_v = 0, 0
    ps7_free = 0
    lat_bank_free = [0, 0]
    spill_done = []

    wait("tensor", s_eye, s_eye.v)
    wait("tensor", sv, consts_done)
    first_router = [True]
    first_lat = [True]

    for tt in range(nT):
        b = tt % 2
        gi = tt // 4
        tg = tt % 4
        wsl = gi % 2
        wait("sync", sa, xt_free[b])
        vx = dma("sync", s_xt[b], xt[b][:, :], x_d[tt * 128:(tt + 1) * 128, :])
        wait("vector", s_xt[b], vx)
        vop("bn_stats", st6[:, 0:6], xt[b][:, 0:512])
        vop("bn_stats", st6[:, 6:12], xt[b][:, 512:1024])
        vop("bn_aggr", mvA[:, :], st6[:, 0:6])
        v_ag = vop("bn_aggr", mvB[:, :], st6[:, 6:12], inc=True)
        wait("vector", sv, v_ag)
        vop("tensor_add", msvs[:, :], mvA[:, :], mvB[:, :])
        v_ag = vop("tensor_sub", dmt[:, :], mvA[:, :], mvB[:, :], inc=True)
        wait("vector", sv, v_ag)
        v_ag = vop("tensor_mul", dmt[:, :], dmt[:, :], dmt[:, :], inc=True)
        wait("vector", sv, v_ag)
        v_stats = vop("scalar_tensor_tensor", out=var4[:, :], in0=msvs[:, 1:2],
                      scalar=2.0, in1=dmt[:, 0:1], op0=ALU.mult, op1=ALU.add,
                      inc=True)
        wait("scalar", sv, v_stats)
        a_ln = act(t1[:, :], var4[:, :], AF.Ln, scale=0.25, bias=epsT[:, 0:1],
                   inc=True)
        wait("scalar", sa, a_ln)
        a_rstd = act(rstd[:, :], t1[:, :], AF.Exp, scale=-0.5, inc=True)
        wait("vector", sa, a_rstd)
        v_nmr = vop("tensor_scalar", nmr[:, :], msvs[:, 0:1], rstd[:, 0:1], -0.5,
                    op0=ALU.mult, op1=ALU.mult, inc=True)
        wait("scalar", sv, v_nmr)
        a_xn = act(xnbf[b][:, :], xt[b][:, :], AF.Identity, scale=rstd[:, 0:1],
                   bias=nmr[:, 0:1], inc=True)
        xt_free[b] = a_xn
        PS, PS3 = (PA, PA3) if b == 0 else (PB, PB3)
        wait("tensor", sa, a_xn)
        wait("tensor", sv, tr_bank_free[b])
        for c in range(8):
            mm(PS[:, c * 128:(c + 1) * 128], xnbf[b][:, c * 128:(c + 1) * 128],
               I128[:, :], True, True, inc=(c == 7))
        p_tr = sp.v
        wait("vector", sp, p_tr)
        if tg == 0 and gi >= 2:
            wait("vector", s_spl[wsl], spill_done[gi - 2])
        v_ev = vop("tensor_copy", xnW3[wsl][:, :, tg * 128:(tg + 1) * 128], PS3,
                   inc=True)
        tr_bank_free[b] = v_ev
        wait("tensor", sv, v_ev)
        if first_router[0]:
            wait("tensor", s_wA, wA_all)
            first_router[0] = False
        wait("tensor", sa, ps6_free_a)
        wait("tensor", sv, ps6_free_v)
        for c in range(8):
            mm(PE_[:, 0:NE], xnW[wsl][:, c * 512 + tg * 128: c * 512 + (tg + 1) * 128],
               Wr_s[:, c * NE:(c + 1) * NE], c == 0, False)
        p_rt = mm(PE_[:, 0:NE], ones1[0:1, :], br_bf[0:1, :], False, True, inc=True)
        wait("vector", sp, p_rt)
        v_nm1 = vop("reduce_max", negm1[:, 0:1], PE_[:, 0:NE], axis=AX, negate=True,
                    inc=True)
        ps6_free_v = v_nm1
        wait("scalar", sv, v_nm1)
        a_ex = act(exps[:, :], PE_[:, 0:NE], AF.Exp, bias=negm1[:, 0:1], inc=True)
        ps6_free_a = a_ex
        wait("vector", sa, a_ex)
        v_mx = vop("max", out=mx8[:, :], in_=exps[:, :], inc=True)
        wait("vector", sv, v_mx)
        vop("tensor_scalar", maskg[:, :], exps[:, :], mx8[:, 1:2], None,
            op0=ALU.is_ge)
        v_e1 = vop("tensor_add", den[:, :], mx8[:, 0:1], mx8[:, 1:2], inc=True)
        wait("vector", sv, v_e1)
        vop("tensor_mul", gp_t[:, :], exps[:, :], maskg[:, :])
        v_rd = vop("reciprocal", rden[:, :], den[:, :], inc=True)
        wait("vector", sv, v_rd)
        v_e2 = vop("tensor_scalar", gates_all[:, NE * tt:NE * (tt + 1)], gp_t[:, :],
                   rden[:, 0:1], None, op0=ALU.mult, inc=True)
        wait("vector", sv, v_e2)
        v_gb = vop("tensor_copy", gbf[:, :], gates_all[:, NE * tt:NE * (tt + 1)],
                   inc=True)
        wait("tensor", sv, v_gb)
        wait("tensor", sv, ps7_free)
        p_gt = mm(PF[0:NE, 0:128], gbf[:, :], I128[:, :], True, True, inc=True)
        wait("vector", sp, p_gt)
        ps7_free = vop("tensor_copy", gatesT_pad[0:NE, tt * 128:(tt + 1) * 128],
                       PF[0:NE, 0:128], inc=True)

        if tg == 3:
            for li in range(2):
                PSl = PC if li == 0 else PD
                wait("tensor", sa, lat_bank_free[li])
                for c in range(8):
                    mm(PSl[:, 0:512],
                       Wl_s[:, c * LAT + 128 * li: c * LAT + 128 * li + 128],
                       xnW[wsl][:, c * 512: c * 512 + 512],
                       c == 0, c == 7, inc=(c == 7))
                p_l = sp.v
                wait("scalar", sp, p_l)
                if first_lat[0]:
                    wait("scalar", s_bias, s_bias.v)
                    first_lat[0] = False
                lat_bank_free[li] = act(
                    latT[:, li * S + gi * 512: li * S + (gi + 1) * 512],
                    PSl[:, 0:512], AF.Identity, scale=1.0,
                    bias=blT[:, li:li + 1], inc=True)
            wait("sync", sv, sv.v)
            spill_done.append(
                dma("sync", s_spl[wsl], xnTh3[:, :, gi * 512:(gi + 1) * 512],
                    xnW3[wsl]))

    if debug:
        wait("sync", sv, sv.v)
        dma("sync", s_bias, gdbg, gates_all[:, :])
    flush_phase()
    es_p1.close()

    for eng in ENGS:
        for c in (sp, sv, sa):
            wait(eng, c, c.v)
        for i in range(2):
            wait(eng, s_spl[i], s_spl[i].v)

    pv_bank = {0: PC, 1: PD}
    sc_bank = {0: PA, 1: PB}

    wo_psum = {0: PE_, 1: PF}

    acc_written = 0

    for g in range(2):
        if g == 1:
            for eng in ENGS:
                for c in (sp, sv, sa):
                    wait(eng, c, c.v)
                for i in range(2):
                    wait(eng, s_ac[i], s_ac[i].v)
        wv0 = dma("gpsimd", s_wg,
                  Wq_g[:, :].rearrange("p (c m) -> p c m", c=8),
                  Wq_d.rearrange("(c p) m -> p c m", p=128)[:, :, 512 * g:512 * (g + 1)])
        dma("gpsimd", s_wg,
            Wo_g[:, :].rearrange("p (c m) -> p c m", c=4),
            Wo_d.rearrange("(c p) m -> p c m", p=128)[:, 4 * g:4 * (g + 1), :])
        dma("gpsimd", s_wg,
            Wk_g[:, :].rearrange("p (c m) -> p c m", c=2),
            Wk_d.rearrange("(c p) m -> p c m", p=128)[:, :, 512 * g:512 * (g + 1)])
        wv3 = dma("gpsimd", s_wg,
                  Wv_g[:, :].rearrange("p (c m) -> p c m", c=2),
                  Wv_d.rearrange("(c p) m -> p c m", p=128)[:, :, 512 * g:512 * (g + 1)])

        wait("tensor", s_wg, wv3)
        kb_free = [0, 0]
        for hp in range(4):
            for jc in range(S // 512):
                bb = (hp * (S // 512) + jc) % 2
                PSl = PC if bb == 0 else PD
                wait("tensor", sa, kb_free[bb])
                for li in range(2):
                    mm(PSl[:, 0:512],
                       Wk_g[:, li * 512 + hp * 128: li * 512 + (hp + 1) * 128],
                       latT[:, li * S + jc * 512: li * S + (jc + 1) * 512],
                       li == 0, li == 1, inc=(li == 1))
                wait("scalar", sp, sp.v)
                kb_free[bb] = act(
                    kT_g[:, hp * S + jc * 512: hp * S + (jc + 1) * 512],
                    PSl[:, 0:512], AF.Identity, scale=1.0,
                    bias=bkT[:, 4 * g + hp:4 * g + hp + 1], inc=True)

        if debug:
            wait("sync", sa, sa.v)
            dma("sync", s_dbg, kdbg[:, g * 4 * S:(g + 1) * 4 * S], kT_g[:, :])
        v_ones = vop("memset", v_aug[:, :], 1.0, inc=True)
        wait("tensor", sv, v_ones)
        vb_free = [0, 0]
        for jt in range(nJB):
            bb = jt % 2
            PSl = PC if bb == 0 else PD
            wait("tensor", sv, vb_free[bb])
            wait("tensor", sa, kb_free[bb])
            for li in range(2):
                mm(PSl[:, 0:512],
                   latT[:, li * S + jt * 128: li * S + (jt + 1) * 128],
                   Wv_g[:, li * 512: (li + 1) * 512],
                   li == 0, False)
            mm(PSl[:, 0:512], ones1[0:1, :], bv_bf[0:1, 512 * g:512 * (g + 1)],
               False, True, inc=True)
            wait("vector", sp, sp.v)
            vb_free[bb] = vop(
                "tensor_copy", v_aug3[:, jt, :, 0:64],
                PSl[:, 0:512].rearrange("p (h dd) -> p h dd", h=8), inc=True)

        if debug:
            wait("sync", sv, sv.v)
            dma("sync", s_dbg, vdbg[:, g * (S // 128) * 520:(g + 1) * (S // 128) * 520],
                v_aug[:, :])
        pe_free_a = 0
        pe_free_v = 0
        nm_free = [0, 0]
        for hp in range(4):
            for jc in range(S // 512):
                a_sq = vop("tensor_mul", sqs[:, :],
                           kT_g[:, hp * S + jc * 512: hp * S + (jc + 1) * 512],
                           kT_g[:, hp * S + jc * 512: hp * S + (jc + 1) * 512],
                           inc=True)
                for h2 in range(2):
                    ind = ind_lo if h2 == 0 else ind_hi
                    kr8 = krow8a if h2 == 0 else krow8b
                    PSn = PC if h2 == 0 else PD
                    wait("tensor", sv, a_sq)
                    wait("tensor", sv, nm_free[h2])
                    p_n = mm(PSn[0:1, 0:512], ind[:, 0:1], sqs[:, :], True, True,
                             inc=True)
                    wait("vector", sp, p_n)
                    nm_free[h2] = vop("reduce_max", kr8[0:1, jc:jc + 1],
                                      PSn[0:1, 0:512], axis=AX, inc=True)
            pe_free_v = max(nm_free)
            wait("vector", sv, pe_free_v)
            vop("reduce_max", kn1[0:1, 2 * hp:2 * hp + 1], krow8a[0:1, 0:S // 512],
                axis=AX)
            vop("reduce_max", kn1[0:1, 2 * hp + 1:2 * hp + 2], krow8b[0:1, 0:S // 512],
                axis=AX)
        v_kn = vop("tensor_copy", t8[0:1, 0:8], kn1[0:1, 0:8], inc=True)
        wait("vector", sv, v_kn)

        wait("tensor", sv, vb_free[0])
        wait("tensor", sv, vb_free[1])
        qb_ld_free = 0
        q_first = True
        wo_free = [0, 0]
        pvb_free = [[0, 0], [0, 0]]
        bc_free = [[0, 0], [0, 0]]
        for qb in range(nQB):
            qs_sem = s_qb[qb % 2]
            wait("sync", sp, qb_ld_free)
            vq = dma("sync", qs_sem, xnT_qb[:, :].rearrange("p (c m) -> p c m", c=8),
                     xnTh3[:, :, qb * 512:(qb + 1) * 512])
            wait("tensor", qs_sem, vq)
            if q_first:
                wait("tensor", s_wg, wv0)
                q_first = False
            qp_free_a = [0, 0]
            wait("tensor", sv, sv.v)
            wait("tensor", sa, pe_free_a)
            for hp in range(4):
                PSq = PE_ if hp % 2 == 0 else PF
                wait("tensor", sa, qp_free_a[hp % 2])
                for c in range(8):
                    mm(PSq[:, 0:512],
                       Wq_g[:, c * 512 + hp * 128: c * 512 + (hp + 1) * 128],
                       xnT_qb[:, c * 512:(c + 1) * 512],
                       c == 0, c == 7, inc=(c == 7))
                wait("scalar", sp, sp.v)
                qp_free_a[hp % 2] = act(qT_qb[:, hp * 512:(hp + 1) * 512],
                                        PSq[:, 0:512], AF.Identity, scale=1.0,
                                        bias=bqT[:, 4 * g + hp:4 * g + hp + 1],
                                        inc=True)
                pe_free_a = qp_free_a[hp % 2]
            nm_free = [0, 0]
            for hp in range(4):
                wait("vector", sa, qp_free_a[hp % 2])
                a_sq = vop("tensor_mul", sqs[:, :], qT_qb[:, hp * 512:(hp + 1) * 512],
                           qT_qb[:, hp * 512:(hp + 1) * 512], inc=True)
                for h2 in range(2):
                    hh = hp * 2 + h2
                    ind = ind_lo if h2 == 0 else ind_hi
                    PSn = PC if hh % 2 == 0 else PD
                    wait("tensor", sv, a_sq)
                    wait("tensor", sv, nm_free[hh % 2])
                    p_n = mm(PSn[0:1, 0:512], ind[:, 0:1], sqs[:, :], True, True,
                             inc=True)
                    wait("vector", sp, p_n)
                    nm_free[hh % 2] = vop("reduce_max", qn1[0:1, hh:hh + 1],
                                          PSn[0:1, 0:512], axis=AX, inc=True)
            pe_free_v = max(nm_free)
            wait("vector", sv, pe_free_v)
            v_t8 = vop("tensor_mul", t8[0:1, 0:8], qn1[0:1, 0:8], kn1[0:1, 0:8],
                       inc=True)
            wait("scalar", sv, v_t8)
            a_l8 = act(l8[0:1, 0:8], t8[0:1, 0:8], AF.Ln, inc=True)
            wait("scalar", sa, a_l8)
            a_m8 = act(m8[0:1, 0:8], l8[0:1, 0:8], AF.Exp, scale=0.5,
                       bias=lnS[0:1, 0:1], inc=True)
            wait("vector", sa, a_m8)
            v_e3 = vop("tensor_scalar", negm8[0:1, 0:8], m8[0:1, 0:8], -1.0, None,
                       op0=ALU.mult, inc=True)
            wait("vector", sv, v_e3)
            v_m8 = vop("tensor_copy", negm8b[0:1, 0:8], negm8[0:1, 0:8], inc=True)
            wait("tensor", sv, v_m8)
            wait("tensor", sv, pe_free_v)
            wait("tensor", sa, pe_free_a)
            p_mb = mm(PE_[:, 0:8], ones1[0:1, :], negm8b[0:1, 0:8], True, True,
                      inc=True)
            wait("vector", sp, p_mb)
            v_mr = vop("tensor_copy", Mrep[:, 0:8], PE_[:, 0:8], inc=True)
            pe_free_v = v_mr
            wait("scalar", sv, v_mr)

            if debug:
                wait("sync", sa, sa.v)
                wait("sync", sv, sv.v)
                dma("sync", s_dbg, qdbg[:, (g * nQB + qb) * 2048:(g * nQB + qb + 1) * 2048],
                    qT_qb[:, :])
                dma("sync", s_dbg, mdbg[:, (g * nQB + qb) * 8:(g * nQB + qb + 1) * 8],
                    Mrep[:, 0:8])
            sc_free = [[0, 0], [0, 0]]
            pv_read = [[0, 0], [0, 0]]
            exp_done = [[0, 0], [0, 0]]
            wait("tensor", sv, sv.v)
            wait("tensor", sa, sa.v)
            for hp in range(4):
                pvb = (PC, PD) if hp % 2 == 0 else (PE_, PF)
                for jt in range(nJB + 1):
                    if jt < nJB:
                        sl = jt % 2
                        for h2 in range(2):
                            PSs = sc_bank[h2]
                            wait("tensor", sa, sc_free[h2][sl])
                            base = hp * S + jt * 128
                            lo = kT_g[64 * h2:64 * h2 + 64, base:base + 64]
                            hi = kT_g[64 * h2:64 * h2 + 64, base + 64:base + 128]
                            rq = qT_qb[64 * h2:64 * h2 + 64, hp * 512:(hp + 1) * 512]
                            mm(PSs[0:64, sl * 512:(sl + 1) * 512], lo, rq,
                               True, True, tp=(64 * h2, 0))
                            mm(PSs[64:128, sl * 512:(sl + 1) * 512], hi, rq,
                               True, True, tp=(64 * h2, 64), inc=(h2 == 1))
                        p_sc = sp.v
                        for h2 in range(2):
                            wait("scalar", sp, p_sc)
                            wait("scalar", sp, pv_read[h2][sl])
                            hh = 2 * hp + h2
                            e_v = act(pT[h2][sl][:, :],
                                      sc_bank[h2][:, sl * 512:(sl + 1) * 512],
                                      AF.Exp, bias=Mrep[:, hh:hh + 1], scale=SCALE,
                                      inc=True)
                            exp_done[h2][sl] = e_v
                            sc_free[h2][sl] = e_v
                    jp = jt - 1
                    if jp >= 0:
                        slp = jp % 2
                        for h2 in range(2):
                            PSpv = pvb[h2]
                            wait("tensor", sa, exp_done[h2][slp])
                            if jp == 0:
                                wait("tensor", sv, pvb_free[hp % 2][h2])
                            mm(PSpv[0:65, 0:512],
                               v_aug[:, jp * 520 + 65 * (2 * hp + h2):
                                     jp * 520 + 65 * (2 * hp + h2) + 65],
                               pT[h2][slp][:, :],
                               jp == 0, jp == nJB - 1, inc=True)
                            pv_read[h2][slp] = sp.v
                bcb = (PC, PD) if hp % 2 == 1 else (PE_, PF)
                for h2 in range(2):
                    PSpv = pvb[h2]
                    PSbc = bcb[h2]
                    hh = 2 * hp + h2
                    wait("vector", sp, pv_read[h2][(nJB - 1) % 2])
                    v_rc = vop("reciprocal", rec1[0:1, :], PSpv[64:65, 0:512],
                               inc=True)
                    wait("vector", sv, v_rc)
                    v_rb = vop("tensor_copy", rec1b[0:1, :], rec1[0:1, :], inc=True)
                    wait("tensor", sv, v_rb)
                    wait("tensor", sv, bc_free[(hp + 1) % 2][h2])
                    p_bc = mm(PSbc[0:64, 0:512], ones64[0:1, 0:64], rec1b[0:1, :],
                              True, True, inc=True)
                    wait("vector", sp, p_bc)
                    v_eb = vop("tensor_copy", recb[0:64, :], PSbc[0:64, 0:512],
                               inc=True)
                    bc_free[(hp + 1) % 2][h2] = v_eb
                    wait("vector", sv, v_eb)
                    v_no = vop("tensor_mul",
                               attn_oT[64 * h2:64 * h2 + 64, hp * 512:(hp + 1) * 512],
                               PSpv[0:64, 0:512], recb[0:64, :], inc=True)
                    pvb_free[hp % 2][h2] = v_no
            wait("tensor", sv, sv.v)
            if debug:
                wait("sync", sv, sv.v)
                dma("sync", s_dbg, aodbg[:, (g * nQB + qb) * 2048:(g * nQB + qb + 1) * 2048],
                    attn_oT[:, :])
            for qs in range(4):
                bx = qs % 2
                row0 = qb * 512 + qs * 128
                src_rows = x_d[row0:row0 + 128, :] if g == 0 else \
                    acc_hbm[row0:row0 + 128, :]
                vx2 = dma("sync", s_x2[bx], x2[bx][:, :], src_rows)
                for ob in range(2):
                    PSo = wo_psum[ob]
                    if ob == 0:
                        wait("tensor", sa, pe_free_a)
                        wait("tensor", sv, pe_free_v)
                    wait("tensor", sv, wo_free[ob])
                    for c in range(4):
                        mm(PSo[:, 0:512],
                           attn_oT[:, c * 512 + qs * 128: c * 512 + (qs + 1) * 128],
                           Wo_g[:, c * 1024 + ob * 512: c * 1024 + (ob + 1) * 512],
                           c == 0, (g == 1 and c == 3),
                           inc=(g == 1 and c == 3))
                    if g == 0:
                        mm(PSo[:, 0:512], ones1[0:1, :],
                           bo_bf[0:1, ob * 512:(ob + 1) * 512], False, False)
                        mm(PSo[:, 0:512],
                           gatesT_pad[:, row0:row0 + 128],
                           b2pad[:, ob * 512:(ob + 1) * 512], False, True,
                           inc=True)
                    p_wo = sp.v
                    wait("vector", sp, p_wo)
                    wait("vector", s_x2[bx], vx2)
                    wait("vector", s_ac[bx], s_ac[bx].v)
                    v_acc = vop("tensor_add", acc_st[bx][:, ob * 512:(ob + 1) * 512],
                                PSo[:, 0:512], x2[bx][:, ob * 512:(ob + 1) * 512],
                                inc=True)
                    wo_free[ob] = v_acc
                    if ob == 0:
                        pe_free_v = v_acc
                wait("sync", sv, sv.v)
                dma("sync", s_ac[bx], acc_hbm[row0:row0 + 128, :], acc_st[bx][:, :])
            qb_ld_free = sp.v

    flush_phase()
    es_attn.close()

    G = min(4, nTTH)
    NB = 4 * G
    es_p3 = ExitStack()
    xnT_h = sbuf("xnT_h", [128, 8 * (S // 2)], BF16, es_p3)
    yacc = sbuf("yacc", [128, nTTH * 1024], F32, es_p3)
    W1s = [sbuf(f"W1s{i}", [128, 8 * HID], BF16, es_p3) for i in range(2)]
    W2s = [sbuf(f"W2s{i}", [128, 8 * D], BF16, es_p3) for i in range(2)]
    b1e = [sbuf(f"b1e{i}", [1, HID], BF16, es_p3) for i in range(2)]
    hbuf = [sbuf(f"hbuf{i}", [128, HID], BF16, es_p3) for i in range(NB)]
    st6f = sbuf("st6f", [128, 12], F32, es_p3)
    mvAf = sbuf("mvAf", [128, 2], F32, es_p3)
    mvBf = sbuf("mvBf", [128, 2], F32, es_p3)
    msvsf = sbuf("msvsf", [128, 2], F32, es_p3)
    dmtf = sbuf("dmtf", [128, 2], F32, es_p3)
    ms4 = [sbuf(f"ms4_{i}", [128, G], F32, es_p3) for i in range(2)]
    vs4 = [sbuf(f"vs4_{i}", [128, G], F32, es_p3) for i in range(2)]
    dm4 = [sbuf(f"dm4_{i}", [128, G], F32, es_p3) for i in range(2)]
    var4g = [sbuf(f"var4g{i}", [128, G], F32, es_p3) for i in range(2)]
    rstd4 = [sbuf(f"rstd4_{i}", [128, G], F32, es_p3) for i in range(2)]
    nmr4 = [sbuf(f"nmr4_{i}", [128, G], F32, es_p3) for i in range(2)]

    for eng in ENGS:
        for c in (sp, sv, sa):
            wait(eng, c, c.v)
        for i in range(2):
            wait(eng, s_ac[i], s_ac[i].v)

    half_S = S // 2
    SW = S // 2
    nGRP = nTTH // G
    slots = [(hf, e, g2) for hf in range(2) for e in range(NE) for g2 in range(nGRP)]
    NSLOT = len(slots)
    SPE = nGRP

    w1_used = [0, 0]
    w2_used = [0, 0]
    w1_loaded = [0, 0]
    w2_loaded = [0, 0]
    hb_used = [0] * NB
    hpsum_free = [0, 0]
    psT_free = 0
    ob_free = [0, 0]
    norm_done = {}
    gelu_done = {}
    stats_done = {}
    hmm_done = {}
    trans_done = {}
    w2_last = {}
    xh_loaded = [0, 0]
    yi_loaded = [0, 0]
    h_last_of_half = {}

    def load_w1(j):
        if j >= 2 * NE:
            return
        hf, e = divmod(j, NE)
        slot = j % 2
        wait("gpsimd", sp, w1_used[slot])
        dma("gpsimd", s_we[slot],
            W1s[slot][:, :].rearrange("p (c m) -> p c m", c=8),
            W1_d[e].rearrange("(c p) m -> p c m", p=128))
        w1_loaded[slot] = dma("gpsimd", s_we[slot], b1e[slot][0:1, :],
                              b1_d[e:e + 1, :])

    def load_w2(j):
        if j >= 2 * NE:
            return
        hf, e = divmod(j, NE)
        slot = j % 2
        wait("gpsimd", sp, w2_used[slot])
        w2_loaded[slot] = dma("gpsimd", s_w2[slot],
                              W2s[slot][:, :].rearrange("p (c m) -> p c m", c=8),
                              W2_d[e].rearrange("(c p) m -> p c m", p=128))

    v_xh = dma("sync", s_xh, xnT_h[:, :].rearrange("p (c m) -> p c m", c=8),
               xnTh3[:, :, 0:half_S])
    xh_loaded[0] = v_xh
    yi_loaded[0] = dma("sync", s_yi,
                       yacc[:, :].rearrange("p (q d) -> p q d", q=nTTH),
                       acc3[:, 0:nTTH, :])
    w1s_of = {}
    w2s_of = {}
    load_w1(0); load_w2(0)
    w1s_of[0], w2s_of[0] = w1_loaded[0], w2_loaded[0]
    load_w1(1); load_w2(1)
    w1s_of[1], w2s_of[1] = w1_loaded[1], w2_loaded[1]

    def stage_H(k):
        hf, e, g2 = slots[k]
        j = hf * NE + e
        slot = j % 2
        for tg in range(G):
            tt = g2 * G + tg
            bb = (k * G + tg) % 2
            bidx = (k * G + tg) % NB
            PSh = PA if bb == 0 else PB
            if tg == 0 and g2 == 0:
                wait("tensor", s_we[slot], w1s_of[j])
                if e == 0:
                    wait("tensor", s_xh, xh_loaded[hf])
            wait("tensor", sv, hpsum_free[bb])
            wait("tensor", sp, hb_used[bidx])
            for hh2 in range(2):
                for c in range(8):
                    mm(PSh[:, hh2 * 512:(hh2 + 1) * 512],
                       xnT_h[:, c * SW + tt * 128: c * SW + (tt + 1) * 128],
                       W1s[slot][:, c * HID + hh2 * 512: c * HID + (hh2 + 1) * 512],
                       c == 0, False)
                mm(PSh[:, hh2 * 512:(hh2 + 1) * 512], ones1[0:1, :],
                   b1e[slot][0:1, hh2 * 512:(hh2 + 1) * 512], False, True,
                   inc=(hh2 == 1))
            hmm_done[(k, tg)] = sp.v
            wait("vector", sp, hmm_done[(k, tg)])
            vop("bn_stats", st6f[:, 0:6], PSh[:, 0:512])
            vop("bn_stats", st6f[:, 6:12], PSh[:, 512:1024])
            vop("bn_aggr", mvAf[:, :], st6f[:, 0:6])
            v_ag = vop("bn_aggr", mvBf[:, :], st6f[:, 6:12], inc=True)
            wait("vector", sv, v_ag)
            vop("tensor_add", ms4[k % 2][:, tg:tg + 1], mvAf[:, 0:1], mvBf[:, 0:1])
            vop("tensor_add", vs4[k % 2][:, tg:tg + 1], mvAf[:, 1:2], mvBf[:, 1:2])
            vop("tensor_sub", dm4[k % 2][:, tg:tg + 1], mvAf[:, 0:1], mvBf[:, 0:1])
            hpsum_free[bb] = vop("tensor_copy", hbuf[bidx][:, :], PSh[:, 0:1024],
                                 inc=True)
        kr = k % 2
        v_ag = vop("tensor_mul", dm4[kr][:, :], dm4[kr][:, :], dm4[kr][:, :],
                   inc=True)
        wait("vector", sv, v_ag)
        vop("scalar_tensor_tensor", out=var4g[kr][:, :], in0=vs4[kr][:, :],
            scalar=2.0, in1=dm4[kr][:, :], op0=ALU.mult, op1=ALU.add)
        stats_done[k] = sv.v
        if g2 == nGRP - 1 and e == NE - 1:
            h_last_of_half[hf] = sp.v
        w1_used_update = sp.v
        if g2 == nGRP - 1:
            w1_used[(hf * NE + e) % 2] = w1_used_update

    def stage_RSTD_NORM(k):
        kr = k % 2
        wait("scalar", sv, stats_done[k])
        a_ln = act(nmr4[kr][:, :], var4g[kr][:, :], AF.Ln, scale=0.25,
                   bias=epsT[:, 0:1], inc=True)
        wait("scalar", sa, a_ln)
        a_rs = act(rstd4[kr][:, :], nmr4[kr][:, :], AF.Exp, scale=-0.5, inc=True)
        wait("vector", sa, a_rs)
        v_nm = vop("tensor_mul", nmr4[kr][:, :], ms4[kr][:, :], rstd4[kr][:, :],
                   inc=True)
        wait("vector", sv, v_nm)
        v_nm = vop("tensor_scalar", nmr4[kr][:, :], nmr4[kr][:, :], -0.5, None,
                   op0=ALU.mult, inc=True)
        wait("scalar", sv, v_nm)
        for tg in range(G):
            bidx = (k * G + tg) % NB
            act(hbuf[bidx][:, :], hbuf[bidx][:, :], AF.Identity,
                scale=rstd4[kr][:, tg:tg + 1], bias=nmr4[kr][:, tg:tg + 1],
                inc=(tg == G - 1))
        norm_done[k] = sa.v

    def stage_TW(kk):
        k1 = kk - 2
        k2 = kk - 3
        for tg in range(G):
            if 0 <= k1 < NSLOT:
                bidx = (k1 * G + tg) % NB
                if tg == 0:
                    wait("tensor", sa, norm_done[k1])
                wait("tensor", sv, psT_free_ref[0])
                for c in range(8):
                    PSt = PC if c < 4 else PD
                    mm(PSt[:, (c % 4) * 128:((c % 4) + 1) * 128],
                       hbuf[bidx][:, c * 128:(c + 1) * 128], I128[:, :],
                       True, True, inc=(c == 7))
                trans_done[(k1, tg)] = sp.v
                hf1, e1, g21 = slots[k1]
                wait("vector", sp, trans_done[(k1, tg)])
                for c in range(8):
                    PSt = PC if c < 4 else PD
                    vop("tensor_scalar", hbuf[bidx][:, c * 128:(c + 1) * 128],
                        PSt[:, (c % 4) * 128:((c % 4) + 1) * 128],
                        g1T[:, e1 * 8 + c:e1 * 8 + c + 1],
                        be1T[:, e1 * 8 + c:e1 * 8 + c + 1],
                        op0=ALU.mult, op1=ALU.add, inc=(c == 7))
                psT_free_ref[0] = sv.v
                wait("scalar", sv, sv.v)
                gelu_done[(k1, tg)] = act(hbuf[bidx][:, :], hbuf[bidx][:, :],
                                          AF.Gelu, inc=True)
            if 0 <= k2 < NSLOT:
                hf2, e2, g22 = slots[k2]
                j2 = hf2 * NE + e2
                slot2 = j2 % 2
                tt2 = g22 * G + tg
                gtt2 = hf2 * nTTH + tt2
                bidx2 = (k2 * G + tg) % NB
                wait("tensor", sa, gelu_done[(k2, tg)])
                if tg == 0 and g22 == 0:
                    wait("tensor", s_w2[slot2], w2s_of[j2])
                for ob in range(2):
                    PSo = PE_ if ob == 0 else PF
                    wait("tensor", sv, ob_free[ob])
                    for c in range(8):
                        mm(PSo[:, 0:512],
                           hbuf[bidx2][:, c * 128:(c + 1) * 128],
                           W2s[slot2][:, c * D + ob * 512: c * D + (ob + 1) * 512],
                           c == 0, c == 7, inc=(c == 7))
                    p_o = sp.v
                    wait("vector", sp, p_o)
                    if e2 == 0 and g22 == 0 and tg == 0 and ob == 0:
                        wait("vector", s_yi, yi_loaded[hf2])
                    ob_free[ob] = vop(
                        "scalar_tensor_tensor",
                        out=yacc[:, tt2 * 1024 + ob * 512: tt2 * 1024 + (ob + 1) * 512],
                        in0=PSo[:, 0:512],
                        scalar=gates_all[:, NE * gtt2 + e2:NE * gtt2 + e2 + 1],
                        in1=yacc[:, tt2 * 1024 + ob * 512: tt2 * 1024 + (ob + 1) * 512],
                        op0=ALU.mult, op1=ALU.add, inc=True)
                hb_used[bidx2] = sp.v
                if tg == G - 1:
                    w2_last[j2] = sp.v
                    if g22 == nGRP - 1:
                        w2_used[j2 % 2] = sp.v

    psT_free_ref = [0]

    for kk in range(NSLOT + 3):
        if kk < NSLOT:
            stage_H(kk)
            hf, e, g2 = slots[kk]
            j = hf * NE + e
            if g2 == 0:
                if j + 2 < 2 * NE:
                    pass
            if g2 == nGRP - 1:
                if j + 2 < 2 * NE:
                    load_w1(j + 2)
                    w1s_of[j + 2] = w1_loaded[(j + 2) % 2]
                if j == NE - 1:
                    wait("sync", sp, h_last_of_half[0])
                    xh_loaded[1] = dma(
                        "sync", s_xh,
                        xnT_h[:, :].rearrange("p (c m) -> p c m", c=8),
                        xnTh3[:, :, half_S:2 * half_S])
        if 0 <= kk - 1 < NSLOT:
            stage_RSTD_NORM(kk - 1)
        stage_TW(kk)
        if 0 <= kk - 3 < NSLOT:
            hf2, e2, g22 = slots[kk - 3]
            j2 = hf2 * NE + e2
            if g22 == nGRP - 1:
                if j2 + 2 < 2 * NE:
                    load_w2(j2 + 2)
                    w2s_of[j2 + 2] = w2_loaded[(j2 + 2) % 2]
                if e2 == NE - 1:
                    wait("sync", sv, sv.v)
                    dma("sync", s_out,
                        out3[:, hf2 * nTTH:(hf2 + 1) * nTTH, :],
                        yacc[:, :].rearrange("p (q d) -> p q d", q=nTTH))
                    if hf2 == 0:
                        wait("sync", s_out, 16)
                        yi_loaded[1] = dma(
                            "sync", s_yi,
                            yacc[:, :].rearrange("p (q d) -> p q d", q=nTTH),
                            acc3[:, nTTH:2 * nTTH, :])

    wait("sync", s_out, s_out.v)
    flush_phase()
    es_p3.close()
    es_top.close()
    return nc


_NC_CACHE = {}


def kernel(**inp):
    S = 4096
    ncore = 8
    if S not in _NC_CACHE:
        _NC_CACHE[S] = build_nc(S)
    nc = _NC_CACHE[S]

    f32 = np.float32
    x = np.ascontiguousarray(inp["L0_attn_x"], dtype=f32)
    shared = {
        "Wq": np.ascontiguousarray(inp["L0_attn_Wq"], f32),
        "Wl": np.ascontiguousarray(inp["L0_attn_Wl"], f32),
        "Wk": np.ascontiguousarray(inp["L0_attn_Wk"], f32),
        "Wv": np.ascontiguousarray(inp["L0_attn_Wv"], f32),
        "Wo": np.ascontiguousarray(inp["L0_attn_Wo"], f32),
        "Wr": np.ascontiguousarray(inp["L0_ffn_Wr"], f32),
        "bqT": np.ascontiguousarray(inp["L0_attn_bq"].reshape(8, 128).T, f32),
        "bkT": np.ascontiguousarray(inp["L0_attn_bk"].reshape(8, 128).T, f32),
        "blT": np.ascontiguousarray(inp["L0_attn_bl"].reshape(2, 128).T, f32),
        "bv": np.ascontiguousarray(inp["L0_attn_bv"].reshape(1, D), f32),
        "bo": np.ascontiguousarray(inp["L0_attn_bo"].reshape(1, D), f32),
        "br": np.ascontiguousarray(inp["L0_ffn_br"].reshape(1, NE), f32),
        "W1": np.ascontiguousarray(inp["L0_ffn_W1"], f32),
        "b1": np.ascontiguousarray(inp["L0_ffn_b1"], f32),
        "g1T": np.ascontiguousarray(
            inp["L0_ffn_g1"].reshape(NE, 8, 128).transpose(2, 0, 1).reshape(128, 128), f32),
        "be1T": np.ascontiguousarray(
            inp["L0_ffn_be1"].reshape(NE, 8, 128).transpose(2, 0, 1).reshape(128, 128), f32),
        "W2": np.ascontiguousarray(inp["L0_ffn_W2"], f32),
        "b2": np.ascontiguousarray(inp["L0_ffn_b2"], f32),
        "eye": np.eye(128, dtype=f32),
    }
    in_maps = []
    for c in range(ncore):
        m = dict(shared)
        m["x"] = np.ascontiguousarray(x[c])
        in_maps.append(m)
    res = run_bass_kernel_spmd(nc, in_maps, core_ids=list(range(ncore)))
    out = np.stack([np.asarray(r["out"], dtype=np.float32) for r in res.results],
                   axis=0)
    return out


# version 07
# speedup vs baseline: 1.1585x; 1.1585x over previous
import numpy as np
import ml_dtypes
from contextlib import ExitStack

import concourse.bass as bass
import concourse.mybir as mybir
from concourse.bass_utils import run_bass_kernel_spmd

F32 = mybir.dt.float32
BF16 = mybir.dt.bfloat16
AF = mybir.ActivationFunctionType
ALU = mybir.AluOpType
AX = mybir.AxisListType.X

D = 1024
H = 16
DH = 64
LAT = 256
NE = 16
HID = 1024
SCALE = DH ** -0.5
EPS = 1e-5


class Ctr:
    def __init__(self, nc, es, name):
        self.h = es.enter_context(nc.semaphore(name))
        self.v = 0

    def bump(self, n=1):
        self.v += n
        return self.v


def build_nc(S=4096, debug=False):
    nT = S // 128
    nQB = S // 512
    nJB = S // 128
    nG = S // 512
    nTTH = nT // 2

    nc = bass.Bass("TRN2", target_bir_lowering=False)

    x_d = nc.dram_tensor("x", [S, D], F32, kind="ExternalInput").ap()
    Wq_d = nc.dram_tensor("Wq", [D, D], F32, kind="ExternalInput").ap()
    Wl_d = nc.dram_tensor("Wl", [D, LAT], F32, kind="ExternalInput").ap()
    Wk_d = nc.dram_tensor("Wk", [LAT, D], F32, kind="ExternalInput").ap()
    Wv_d = nc.dram_tensor("Wv", [LAT, D], F32, kind="ExternalInput").ap()
    Wo_d = nc.dram_tensor("Wo", [D, D], F32, kind="ExternalInput").ap()
    Wr_d = nc.dram_tensor("Wr", [D, NE], F32, kind="ExternalInput").ap()
    bqT_d = nc.dram_tensor("bqT", [128, 8], F32, kind="ExternalInput").ap()
    bkT_d = nc.dram_tensor("bkT", [128, 8], F32, kind="ExternalInput").ap()
    blT_d = nc.dram_tensor("blT", [128, 2], F32, kind="ExternalInput").ap()
    bv_d = nc.dram_tensor("bv", [1, D], F32, kind="ExternalInput").ap()
    bo_d = nc.dram_tensor("bo", [1, D], F32, kind="ExternalInput").ap()
    br_d = nc.dram_tensor("br", [1, NE], F32, kind="ExternalInput").ap()
    W1_d = nc.dram_tensor("W1", [NE, D, HID], F32, kind="ExternalInput").ap()
    b1_d = nc.dram_tensor("b1", [NE, HID], F32, kind="ExternalInput").ap()
    g1T_d = nc.dram_tensor("g1T", [128, 128], F32, kind="ExternalInput").ap()
    be1T_d = nc.dram_tensor("be1T", [128, 128], F32, kind="ExternalInput").ap()
    W2_d = nc.dram_tensor("W2", [NE, HID, D], F32, kind="ExternalInput").ap()
    b2_d = nc.dram_tensor("b2", [NE, D], F32, kind="ExternalInput").ap()
    eye_d = nc.dram_tensor("eye", [128, 128], F32, kind="ExternalInput").ap()
    out_d = nc.dram_tensor("out", [S, D], F32, kind="ExternalOutput").ap()
    ikind = "ExternalOutput" if debug else "Internal"
    xnT_hbm = nc.dram_tensor("xnT_hbm", [128, 8 * S], BF16, kind=ikind).ap()
    acc_hbm = nc.dram_tensor("acc_hbm", [S, D], F32, kind=ikind).ap()
    gdbg = nc.dram_tensor("gdbg", [128, NE * (S // 128)], F32,
                          kind="ExternalOutput").ap() if debug else None
    if debug:
        kdbg = nc.dram_tensor("kdbg", [128, 8 * S], BF16, kind="ExternalOutput").ap()
        qdbg = nc.dram_tensor("qdbg", [128, 2 * (S // 512) * 2048], BF16,
                              kind="ExternalOutput").ap()
        vdbg = nc.dram_tensor("vdbg", [128, 2 * (S // 128) * 520], BF16,
                              kind="ExternalOutput").ap()
        aodbg = nc.dram_tensor("aodbg", [128, 2 * (S // 512) * 2048], BF16,
                               kind="ExternalOutput").ap()
        mdbg = nc.dram_tensor("mdbg", [128, 2 * (S // 512) * 8], F32,
                              kind="ExternalOutput").ap()

    xnTh3 = xnT_hbm.rearrange("p (c s) -> p c s", c=8)
    acc3 = acc_hbm.rearrange("(q p) d -> p q d", p=128)
    out3 = out_d.rearrange("(q p) d -> p q d", p=128)

    ENGS = ("sync", "gpsimd", "vector", "scalar", "tensor")
    sched = {e: [] for e in ENGS}

    def em(eng, fn):
        sched[eng].append(fn)

    def flush_phase():
        with nc.Block() as blk:
            @blk.sync
            def _(e):
                for f in sched["sync"]:
                    f(e)

            @blk.gpsimd
            def _(e):
                for f in sched["gpsimd"]:
                    f(e)

            @blk.vector
            def _(e):
                for f in sched["vector"]:
                    f(e)

            @blk.scalar
            def _(e):
                for f in sched["scalar"]:
                    f(e)

            @blk.tensor
            def _(e):
                for f in sched["tensor"]:
                    f(e)
        for e in ENGS:
            sched[e].clear()

    es_top = ExitStack()

    def sbuf(name, shape, dt, stack):
        return stack.enter_context(nc.sbuf_tensor(name, shape, dt))

    sp = Ctr(nc, es_top, "sp")
    sv = Ctr(nc, es_top, "sv")
    sa = Ctr(nc, es_top, "sa")
    s_wA = Ctr(nc, es_top, "s_wA")
    s_bias = Ctr(nc, es_top, "s_bias")
    s_xt = [Ctr(nc, es_top, f"s_xt{i}") for i in range(2)]
    s_spl = [Ctr(nc, es_top, f"s_spl{i}") for i in range(2)]
    s_wg = Ctr(nc, es_top, "s_wg")
    s_qb = [Ctr(nc, es_top, f"s_qb{i}") for i in range(2)]
    s_x2 = [Ctr(nc, es_top, f"s_x2{i}") for i in range(2)]
    s_ac = [Ctr(nc, es_top, f"s_ac{i}") for i in range(2)]
    s_we = [Ctr(nc, es_top, f"s_we{i}") for i in range(2)]
    s_w2 = [Ctr(nc, es_top, f"s_w2{i}") for i in range(2)]
    s_xh = Ctr(nc, es_top, "s_xh")
    s_yi = Ctr(nc, es_top, "s_yi")
    s_out = Ctr(nc, es_top, "s_out")
    s_eye = Ctr(nc, es_top, "s_eye")
    s_dbg = Ctr(nc, es_top, "s_dbg") if debug else None

    def wait(eng, ctr, val):
        if val > 0:
            em(eng, lambda e, c=ctr.h, v=val: e.wait_ge(c, v))

    def mm(out_ap, lhsT, rhs, start, stop, inc=False, tp=None):
        if inc:
            val = sp.bump()
            em("tensor", lambda e, o=out_ap, l=lhsT, r=rhs, s1=start, s2=stop, t=tp:
               e.matmul(o, l, r, start=s1, stop=s2, tile_position=t).then_inc(sp.h, 1))
            return val
        em("tensor", lambda e, o=out_ap, l=lhsT, r=rhs, s1=start, s2=stop, t=tp:
           e.matmul(o, l, r, start=s1, stop=s2, tile_position=t))
        return None

    def act(out_ap, in_ap, func, bias=0.0, scale=1.0, inc=False):
        if inc:
            val = sa.bump()
            em("scalar", lambda e, o=out_ap, i=in_ap, f=func, b=bias, s=scale:
               e.activation(o, i, f, bias=b, scale=s).then_inc(sa.h, 1))
            return val
        em("scalar", lambda e, o=out_ap, i=in_ap, f=func, b=bias, s=scale:
           e.activation(o, i, f, bias=b, scale=s))
        return None

    def vop(name, *args, inc=False, **kw):
        if inc:
            val = sv.bump()
            em("vector", lambda e, n=name, a=args, k=kw:
               getattr(e, n)(*a, **k).then_inc(sv.h, 1))
            return val
        em("vector", lambda e, n=name, a=args, k=kw: getattr(e, n)(*a, **k))
        return None

    def dma(eng, ctr, out_ap, in_ap):
        val = ctr.bump(16)
        em(eng, lambda e, o=out_ap, i=in_ap, c=ctr.h:
           e.dma_start(o, i).then_inc(c, 16))
        return val

    ones1 = sbuf("ones1", [1, 128], BF16, es_top)
    ones64 = sbuf("ones64", [1, 64], BF16, es_top)
    ind_lo = sbuf("ind_lo", [128, 1], BF16, es_top)
    ind_hi = sbuf("ind_hi", [128, 1], BF16, es_top)
    I128 = sbuf("I128", [128, 128], BF16, es_top)
    epsT = sbuf("epsT", [128, 1], F32, es_top)
    lnS = sbuf("lnS", [1, 1], F32, es_top)
    g1T = sbuf("g1Ts", [128, 128], F32, es_top)
    be1T = sbuf("be1Ts", [128, 128], F32, es_top)
    gates_all = sbuf("gates_all", [128, NE * nT], F32, es_top)

    PA = es_top.enter_context(nc.psum_tensor("PA", [128, 1024], F32))
    PB = es_top.enter_context(nc.psum_tensor("PB", [128, 1024], F32))
    PC = es_top.enter_context(nc.psum_tensor("PC", [128, 512], F32))
    PD = es_top.enter_context(nc.psum_tensor("PD", [128, 512], F32))
    PE_ = es_top.enter_context(nc.psum_tensor("PE", [128, 512], F32))
    PF = es_top.enter_context(nc.psum_tensor("PF", [128, 512], F32))
    PA3 = PA[:, :].rearrange("p (c t) -> p c t", c=8)
    PB3 = PB[:, :].rearrange("p (c t) -> p c t", c=8)

    dma("gpsimd", s_eye, I128[:, :], eye_d)
    for dst, src in ((g1T[:, :], g1T_d), (be1T[:, :], be1T_d)):
        dma("sync", s_bias, dst, src)
    vop("memset", ones1[:, :], 1.0)
    vop("memset", ones64[:, :], 1.0)
    vop("memset", ind_lo[0:64, :], 1.0)
    vop("memset", ind_lo[64:128, :], 0.0)
    vop("memset", ind_hi[0:64, :], 0.0)
    vop("memset", ind_hi[64:128, :], 1.0)
    vop("memset", epsT[:, :], EPS)
    consts_done = vop("memset", lnS[:, :], float(np.log(SCALE)), inc=True)

    es_attn = ExitStack()
    bqT = sbuf("bqTs", [128, 8], F32, es_attn)
    bkT = sbuf("bkTs", [128, 8], F32, es_attn)
    blT = sbuf("blTs", [128, 2], F32, es_attn)
    bv_bf = sbuf("bv_bf", [1, D], BF16, es_attn)
    bo_bf = sbuf("bo_bf", [1, D], BF16, es_attn)
    br_bf = sbuf("br_bf", [1, NE], BF16, es_attn)
    b2pad = sbuf("b2pad", [128, D], BF16, es_attn)
    gatesT_pad = sbuf("gatesT_pad", [128, S], BF16, es_attn)
    Wr_s = sbuf("Wr_s", [128, 8 * NE], BF16, es_attn)
    latT = sbuf("latT", [128, 2 * S], BF16, es_attn)
    for dst, srcp in (
        (bv_bf[0:1, :], bv_d),
        (bo_bf[0:1, :], bo_d),
        (br_bf[0:1, :], br_d),
        (Wr_s[:, :].rearrange("p (c m) -> p c m", c=8),
         Wr_d.rearrange("(c p) m -> p c m", p=128)),
    ):
        dma("gpsimd", s_wA, dst, srcp)
    for dst, srcp in ((bqT[:, :], bqT_d), (bkT[:, :], bkT_d), (blT[:, :], blT_d)):
        dma("sync", s_bias, dst, srcp)
    vop("memset", gatesT_pad[:, :], 0.0)
    consts2_done = vop("memset", b2pad[:, :], 0.0, inc=True)
    wait("gpsimd", sv, consts2_done)
    dma("gpsimd", s_wA, b2pad[0:16, :], b2_d)
    kT_g = sbuf("kT_g", [128, 4 * S], BF16, es_attn)
    v_aug = sbuf("v_aug", [128, nJB * 520], BF16, es_attn)
    Wl_s = sbuf("Wl_s", [128, 8 * LAT], BF16, es_attn)
    Wq_g = sbuf("Wq_g", [128, 8 * 512], BF16, es_attn)
    Wo_g = sbuf("Wo_g", [128, 4 * 1024], BF16, es_attn)
    Wk_g = sbuf("Wk_g", [128, 2 * 512], BF16, es_attn)
    Wv_g = sbuf("Wv_g", [128, 2 * 512], BF16, es_attn)
    xnT_qb = sbuf("xnT_qb", [128, 8 * 512], BF16, es_attn)
    qT_qb = sbuf("qT_qb", [128, 4 * 512], BF16, es_attn)
    pT = [[sbuf(f"pT{h}_{i}", [128, 512], BF16, es_attn) for i in range(2)]
          for h in range(2)]
    attn_oT = sbuf("attn_oT", [128, 4 * 512], BF16, es_attn)
    sqs = [sbuf(f"sqs{i}", [128, 512], BF16, es_attn) for i in range(2)]
    krow8a = sbuf("krow8a", [1, 8], F32, es_attn)
    krow8b = sbuf("krow8b", [1, 8], F32, es_attn)
    kn1 = sbuf("kn1", [1, 8], F32, es_attn)
    qn1 = sbuf("qn1", [1, 8], F32, es_attn)
    t8 = sbuf("t8", [1, 8], F32, es_attn)
    l8 = sbuf("l8", [1, 8], F32, es_attn)
    m8 = sbuf("m8", [1, 8], F32, es_attn)
    negm8 = sbuf("negm8", [1, 8], F32, es_attn)
    negm8b = sbuf("negm8b", [1, 8], BF16, es_attn)
    Mrep = sbuf("Mrep", [128, 8], F32, es_attn)
    rec1 = sbuf("rec1", [1, 512], F32, es_attn)
    rec1b = sbuf("rec1b", [1, 512], BF16, es_attn)
    recb = sbuf("recb", [64, 512], F32, es_attn)
    x2 = [sbuf(f"x2_{i}", [128, D], F32, es_attn) for i in range(2)]
    acc_st = [sbuf(f"acc_st{i}", [128, D], F32, es_attn) for i in range(2)]
    v_aug3 = v_aug[:, :].rearrange("p (j h dd) -> p j h dd", h=8, dd=65)

    es_p1 = ExitStack()
    xt = [sbuf(f"xt{i}", [128, D], F32, es_p1) for i in range(2)]
    xnbf = [sbuf(f"xnbf{i}", [128, D], BF16, es_p1) for i in range(2)]
    xnW = [sbuf(f"xnW{i}", [128, 8 * 512], BF16, es_p1) for i in range(2)]
    st6 = sbuf("st6", [128, 12], F32, es_p1)
    mvA = sbuf("mvA", [128, 2], F32, es_p1)
    mvB = sbuf("mvB", [128, 2], F32, es_p1)
    msvs = sbuf("msvs", [128, 2], F32, es_p1)
    dmt = sbuf("dmt", [128, 2], F32, es_p1)
    var4 = sbuf("var4", [128, 1], F32, es_p1)
    t1 = sbuf("t1", [128, 1], F32, es_p1)
    rstd = sbuf("rstd", [128, 1], F32, es_p1)
    nmr = sbuf("nmr", [128, 1], F32, es_p1)
    negm1 = sbuf("negm1", [128, 1], F32, es_p1)
    exps = sbuf("exps", [128, NE], F32, es_p1)
    mx8 = sbuf("mx8", [128, 8], F32, es_p1)
    den = sbuf("den", [128, 1], F32, es_p1)
    rden = sbuf("rden", [128, 1], F32, es_p1)
    maskg = sbuf("maskg", [128, NE], F32, es_p1)
    gp_t = sbuf("gp_t", [128, NE], F32, es_p1)
    gbf = sbuf("gbf", [128, NE], BF16, es_p1)
    xnW3 = [w[:, :].rearrange("p (c t) -> p c t", c=8) for w in xnW]

    dma("gpsimd", s_wA, Wl_s[:, :].rearrange("p (c m) -> p c m", c=8),
        Wl_d.rearrange("(c p) m -> p c m", p=128))
    wA_all = s_wA.v

    xt_free = [0, 0]
    tr_bank_free = [0, 0]
    ps6_free_a, ps6_free_v = 0, 0
    ps7_free = 0
    lat_bank_free = [0, 0]
    spill_done = []

    wait("tensor", s_eye, s_eye.v)
    wait("tensor", sv, consts_done)
    first_router = [True]
    first_lat = [True]

    for tt in range(nT):
        b = tt % 2
        gi = tt // 4
        tg = tt % 4
        wsl = gi % 2
        wait("sync", sa, xt_free[b])
        vx = dma("sync", s_xt[b], xt[b][:, :], x_d[tt * 128:(tt + 1) * 128, :])
        wait("vector", s_xt[b], vx)
        vop("bn_stats", st6[:, 0:6], xt[b][:, 0:512])
        vop("bn_stats", st6[:, 6:12], xt[b][:, 512:1024])
        vop("bn_aggr", mvA[:, :], st6[:, 0:6])
        v_ag = vop("bn_aggr", mvB[:, :], st6[:, 6:12], inc=True)
        wait("vector", sv, v_ag)
        vop("tensor_add", msvs[:, :], mvA[:, :], mvB[:, :])
        v_ag = vop("tensor_sub", dmt[:, :], mvA[:, :], mvB[:, :], inc=True)
        wait("vector", sv, v_ag)
        v_ag = vop("tensor_mul", dmt[:, :], dmt[:, :], dmt[:, :], inc=True)
        wait("vector", sv, v_ag)
        v_stats = vop("scalar_tensor_tensor", out=var4[:, :], in0=msvs[:, 1:2],
                      scalar=2.0, in1=dmt[:, 0:1], op0=ALU.mult, op1=ALU.add,
                      inc=True)
        wait("scalar", sv, v_stats)
        a_ln = act(t1[:, :], var4[:, :], AF.Ln, scale=0.25, bias=epsT[:, 0:1],
                   inc=True)
        wait("scalar", sa, a_ln)
        a_rstd = act(rstd[:, :], t1[:, :], AF.Exp, scale=-0.5, inc=True)
        wait("vector", sa, a_rstd)
        v_nmr = vop("tensor_scalar", nmr[:, :], msvs[:, 0:1], rstd[:, 0:1], -0.5,
                    op0=ALU.mult, op1=ALU.mult, inc=True)
        wait("scalar", sv, v_nmr)
        a_xn = act(xnbf[b][:, :], xt[b][:, :], AF.Identity, scale=rstd[:, 0:1],
                   bias=nmr[:, 0:1], inc=True)
        xt_free[b] = a_xn
        PS, PS3 = (PA, PA3) if b == 0 else (PB, PB3)
        wait("tensor", sa, a_xn)
        wait("tensor", sv, tr_bank_free[b])
        for c in range(8):
            mm(PS[:, c * 128:(c + 1) * 128], xnbf[b][:, c * 128:(c + 1) * 128],
               I128[:, :], True, True, inc=(c == 7))
        p_tr = sp.v
        wait("vector", sp, p_tr)
        if tg == 0 and gi >= 2:
            wait("vector", s_spl[wsl], spill_done[gi - 2])
        v_ev = vop("tensor_copy", xnW3[wsl][:, :, tg * 128:(tg + 1) * 128], PS3,
                   inc=True)
        tr_bank_free[b] = v_ev
        wait("tensor", sv, v_ev)
        if first_router[0]:
            wait("tensor", s_wA, wA_all)
            first_router[0] = False
        wait("tensor", sa, ps6_free_a)
        wait("tensor", sv, ps6_free_v)
        for c in range(8):
            mm(PE_[:, 0:NE], xnW[wsl][:, c * 512 + tg * 128: c * 512 + (tg + 1) * 128],
               Wr_s[:, c * NE:(c + 1) * NE], c == 0, False)
        p_rt = mm(PE_[:, 0:NE], ones1[0:1, :], br_bf[0:1, :], False, True, inc=True)
        wait("vector", sp, p_rt)
        v_nm1 = vop("reduce_max", negm1[:, 0:1], PE_[:, 0:NE], axis=AX, negate=True,
                    inc=True)
        ps6_free_v = v_nm1
        wait("scalar", sv, v_nm1)
        a_ex = act(exps[:, :], PE_[:, 0:NE], AF.Exp, bias=negm1[:, 0:1], inc=True)
        ps6_free_a = a_ex
        wait("vector", sa, a_ex)
        v_mx = vop("max", out=mx8[:, :], in_=exps[:, :], inc=True)
        wait("vector", sv, v_mx)
        vop("tensor_scalar", maskg[:, :], exps[:, :], mx8[:, 1:2], None,
            op0=ALU.is_ge)
        v_e1 = vop("tensor_add", den[:, :], mx8[:, 0:1], mx8[:, 1:2], inc=True)
        wait("vector", sv, v_e1)
        vop("tensor_mul", gp_t[:, :], exps[:, :], maskg[:, :])
        v_rd = vop("reciprocal", rden[:, :], den[:, :], inc=True)
        wait("vector", sv, v_rd)
        v_e2 = vop("tensor_scalar", gates_all[:, NE * tt:NE * (tt + 1)], gp_t[:, :],
                   rden[:, 0:1], None, op0=ALU.mult, inc=True)
        wait("vector", sv, v_e2)
        v_gb = vop("tensor_copy", gbf[:, :], gates_all[:, NE * tt:NE * (tt + 1)],
                   inc=True)
        wait("tensor", sv, v_gb)
        wait("tensor", sv, ps7_free)
        p_gt = mm(PF[0:NE, 0:128], gbf[:, :], I128[:, :], True, True, inc=True)
        wait("vector", sp, p_gt)
        ps7_free = vop("tensor_copy", gatesT_pad[0:NE, tt * 128:(tt + 1) * 128],
                       PF[0:NE, 0:128], inc=True)

        if tg == 3:
            for li in range(2):
                PSl = PC if li == 0 else PD
                wait("tensor", sa, lat_bank_free[li])
                for c in range(8):
                    mm(PSl[:, 0:512],
                       Wl_s[:, c * LAT + 128 * li: c * LAT + 128 * li + 128],
                       xnW[wsl][:, c * 512: c * 512 + 512],
                       c == 0, c == 7, inc=(c == 7))
                p_l = sp.v
                wait("scalar", sp, p_l)
                if first_lat[0]:
                    wait("scalar", s_bias, s_bias.v)
                    first_lat[0] = False
                lat_bank_free[li] = act(
                    latT[:, li * S + gi * 512: li * S + (gi + 1) * 512],
                    PSl[:, 0:512], AF.Identity, scale=1.0,
                    bias=blT[:, li:li + 1], inc=True)
            wait("sync", sv, sv.v)
            spill_done.append(
                dma("sync", s_spl[wsl], xnTh3[:, :, gi * 512:(gi + 1) * 512],
                    xnW3[wsl]))

    if debug:
        wait("sync", sv, sv.v)
        dma("sync", s_bias, gdbg, gates_all[:, :])
    flush_phase()
    es_p1.close()

    for eng in ENGS:
        for c in (sp, sv, sa):
            wait(eng, c, c.v)
        for i in range(2):
            wait(eng, s_spl[i], s_spl[i].v)

    pv_bank = {0: PC, 1: PD}
    sc_bank = {0: PA, 1: PB}

    wo_psum = {0: PE_, 1: PF}

    acc_written = 0

    for g in range(2):
        if g == 1:
            for eng in ENGS:
                for c in (sp, sv, sa):
                    wait(eng, c, c.v)
                for i in range(2):
                    wait(eng, s_ac[i], s_ac[i].v)
        wv0 = dma("gpsimd", s_wg,
                  Wq_g[:, :].rearrange("p (c m) -> p c m", c=8),
                  Wq_d.rearrange("(c p) m -> p c m", p=128)[:, :, 512 * g:512 * (g + 1)])
        dma("gpsimd", s_wg,
            Wo_g[:, :].rearrange("p (c m) -> p c m", c=4),
            Wo_d.rearrange("(c p) m -> p c m", p=128)[:, 4 * g:4 * (g + 1), :])
        dma("gpsimd", s_wg,
            Wk_g[:, :].rearrange("p (c m) -> p c m", c=2),
            Wk_d.rearrange("(c p) m -> p c m", p=128)[:, :, 512 * g:512 * (g + 1)])
        wv3 = dma("gpsimd", s_wg,
                  Wv_g[:, :].rearrange("p (c m) -> p c m", c=2),
                  Wv_d.rearrange("(c p) m -> p c m", p=128)[:, :, 512 * g:512 * (g + 1)])

        wait("tensor", s_wg, wv3)
        kb_free = [0, 0]
        for hp in range(4):
            for jc in range(S // 512):
                bb = (hp * (S // 512) + jc) % 2
                PSl = PC if bb == 0 else PD
                wait("tensor", sa, kb_free[bb])
                for li in range(2):
                    mm(PSl[:, 0:512],
                       Wk_g[:, li * 512 + hp * 128: li * 512 + (hp + 1) * 128],
                       latT[:, li * S + jc * 512: li * S + (jc + 1) * 512],
                       li == 0, li == 1, inc=(li == 1))
                wait("scalar", sp, sp.v)
                kb_free[bb] = act(
                    kT_g[:, hp * S + jc * 512: hp * S + (jc + 1) * 512],
                    PSl[:, 0:512], AF.Identity, scale=1.0,
                    bias=bkT[:, 4 * g + hp:4 * g + hp + 1], inc=True)

        if debug:
            wait("sync", sa, sa.v)
            dma("sync", s_dbg, kdbg[:, g * 4 * S:(g + 1) * 4 * S], kT_g[:, :])
        v_ones = vop("memset", v_aug[:, :], 1.0, inc=True)
        wait("tensor", sv, v_ones)
        vb_free = [0, 0]
        for jt in range(nJB):
            bb = jt % 2
            PSl = PC if bb == 0 else PD
            wait("tensor", sv, vb_free[bb])
            wait("tensor", sa, kb_free[bb])
            for li in range(2):
                mm(PSl[:, 0:512],
                   latT[:, li * S + jt * 128: li * S + (jt + 1) * 128],
                   Wv_g[:, li * 512: (li + 1) * 512],
                   li == 0, False)
            mm(PSl[:, 0:512], ones1[0:1, :], bv_bf[0:1, 512 * g:512 * (g + 1)],
               False, True, inc=True)
            wait("vector", sp, sp.v)
            vb_free[bb] = vop(
                "tensor_copy", v_aug3[:, jt, :, 0:64],
                PSl[:, 0:512].rearrange("p (h dd) -> p h dd", h=8), inc=True)

        if debug:
            wait("sync", sv, sv.v)
            dma("sync", s_dbg, vdbg[:, g * (S // 128) * 520:(g + 1) * (S // 128) * 520],
                v_aug[:, :])
        pe_free_a = 0
        pe_free_v = 0
        nm_free = [0, 0]
        sq_done = {}
        nJC = S // 512
        for hp in range(4):
            for jc in range(nJC):
                sq_done[jc] = vop(
                    "tensor_mul", sqs[jc % 2][:, :],
                    kT_g[:, hp * S + jc * 512: hp * S + (jc + 1) * 512],
                    kT_g[:, hp * S + jc * 512: hp * S + (jc + 1) * 512],
                    inc=(True))
                if jc >= 1:
                    jp = jc - 1
                    for h2 in range(2):
                        ind = ind_lo if h2 == 0 else ind_hi
                        kr8 = krow8a if h2 == 0 else krow8b
                        PSn = PC if h2 == 0 else PD
                        wait("tensor", sv, sq_done[jp])
                        wait("tensor", sv, nm_free[h2])
                        p_n = mm(PSn[0:1, 0:512], ind[:, 0:1], sqs[jp % 2][:, :],
                                 True, True, inc=True)
                        wait("vector", sp, p_n)
                        nm_free[h2] = vop("reduce_max", kr8[0:1, jp:jp + 1],
                                          PSn[0:1, 0:512], axis=AX, inc=True)
            jp = nJC - 1
            for h2 in range(2):
                ind = ind_lo if h2 == 0 else ind_hi
                kr8 = krow8a if h2 == 0 else krow8b
                PSn = PC if h2 == 0 else PD
                wait("tensor", sv, sq_done[jp])
                wait("tensor", sv, nm_free[h2])
                p_n = mm(PSn[0:1, 0:512], ind[:, 0:1], sqs[jp % 2][:, :],
                         True, True, inc=True)
                wait("vector", sp, p_n)
                nm_free[h2] = vop("reduce_max", kr8[0:1, jp:jp + 1],
                                  PSn[0:1, 0:512], axis=AX, inc=True)
            pe_free_v = max(nm_free)
            wait("vector", sv, pe_free_v)
            vop("reduce_max", kn1[0:1, 2 * hp:2 * hp + 1], krow8a[0:1, 0:nJC],
                axis=AX)
            vop("reduce_max", kn1[0:1, 2 * hp + 1:2 * hp + 2], krow8b[0:1, 0:nJC],
                axis=AX)
        v_kn = vop("tensor_copy", t8[0:1, 0:8], kn1[0:1, 0:8], inc=True)
        wait("vector", sv, v_kn)
        knorm_free = [nm_free[0], nm_free[1]]

        wait("tensor", sv, vb_free[0])
        wait("tensor", sv, vb_free[1])
        qb_ld_free = 0
        q_first = True
        wo_free = [0, 0]
        pvb_free = [[0, 0], [0, 0]]
        bc_free = [[0, 0], [0, 0]]
        for qb in range(nQB):
            qs_sem = s_qb[qb % 2]
            wait("sync", sp, qb_ld_free)
            vq = dma("sync", qs_sem, xnT_qb[:, :].rearrange("p (c m) -> p c m", c=8),
                     xnTh3[:, :, qb * 512:(qb + 1) * 512])
            wait("tensor", qs_sem, vq)
            if q_first:
                wait("tensor", s_wg, wv0)
                q_first = False
            qp_done = [0, 0, 0, 0]
            for hp in range(4):
                PSq = PC if hp % 2 == 0 else PD
                if qb == 0:
                    wait("tensor", sv, knorm_free[hp % 2])
                wait("tensor", sv, pvb_free[0][hp % 2])
                wait("tensor", sv, bc_free[0][hp % 2])
                if hp >= 2:
                    wait("tensor", sa, qp_done[hp - 2])
                for c in range(8):
                    mm(PSq[:, 0:512],
                       Wq_g[:, c * 512 + hp * 128: c * 512 + (hp + 1) * 128],
                       xnT_qb[:, c * 512:(c + 1) * 512],
                       c == 0, c == 7, inc=(c == 7))
                wait("scalar", sp, sp.v)
                qp_done[hp] = act(qT_qb[:, hp * 512:(hp + 1) * 512],
                                  PSq[:, 0:512], AF.Identity, scale=1.0,
                                  bias=bqT[:, 4 * g + hp:4 * g + hp + 1],
                                  inc=True)
            pe_free_a = qp_done[3]
            nm_free = [0, 0]
            sqq_done = [0, 0, 0, 0]
            sq_read = [0, 0, 0, 0]
            for hp in range(4):
                wait("vector", sa, qp_done[hp])
                if hp >= 2:
                    wait("vector", sp, sq_read[hp - 2])
                sqq_done[hp] = vop("tensor_mul", sqs[hp % 2][:, :],
                                   qT_qb[:, hp * 512:(hp + 1) * 512],
                                   qT_qb[:, hp * 512:(hp + 1) * 512], inc=True)
            for hp in range(4):
                for h2 in range(2):
                    hh = hp * 2 + h2
                    ind = ind_lo if h2 == 0 else ind_hi
                    PSn = PE_ if hh % 2 == 0 else PF
                    wait("tensor", sv, sqq_done[hp])
                    if hh < 2:
                        wait("tensor", sv, pvb_free[1][hh % 2])
                        wait("tensor", sv, bc_free[1][hh % 2])
                        wait("tensor", sv, wo_free[hh % 2])
                    wait("tensor", sv, nm_free[hh % 2])
                    p_n = mm(PSn[0:1, 0:512], ind[:, 0:1], sqs[hp % 2][:, :],
                             True, True, inc=True)
                    wait("vector", sp, p_n)
                    sq_read[hp] = sp.v
                    nm_free[hh % 2] = vop("reduce_max", qn1[0:1, hh:hh + 1],
                                          PSn[0:1, 0:512], axis=AX, inc=True)
            pe_free_v = max(nm_free)
            wait("vector", sv, pe_free_v)
            v_t8 = vop("tensor_mul", t8[0:1, 0:8], qn1[0:1, 0:8], kn1[0:1, 0:8],
                       inc=True)
            wait("scalar", sv, v_t8)
            a_l8 = act(l8[0:1, 0:8], t8[0:1, 0:8], AF.Ln, inc=True)
            wait("scalar", sa, a_l8)
            a_m8 = act(m8[0:1, 0:8], l8[0:1, 0:8], AF.Exp, scale=0.5,
                       bias=lnS[0:1, 0:1], inc=True)
            wait("vector", sa, a_m8)
            v_e3 = vop("tensor_scalar", negm8[0:1, 0:8], m8[0:1, 0:8], -1.0, None,
                       op0=ALU.mult, inc=True)
            wait("vector", sv, v_e3)
            v_m8 = vop("tensor_copy", negm8b[0:1, 0:8], negm8[0:1, 0:8], inc=True)
            wait("tensor", sv, v_m8)
            wait("tensor", sv, pe_free_v)
            wait("tensor", sa, pe_free_a)
            p_mb = mm(PE_[:, 0:8], ones1[0:1, :], negm8b[0:1, 0:8], True, True,
                      inc=True)
            wait("vector", sp, p_mb)
            v_mr = vop("tensor_copy", Mrep[:, 0:8], PE_[:, 0:8], inc=True)
            pe_free_v = v_mr
            wait("scalar", sv, v_mr)

            if debug:
                wait("sync", sa, sa.v)
                wait("sync", sv, sv.v)
                dma("sync", s_dbg, qdbg[:, (g * nQB + qb) * 2048:(g * nQB + qb + 1) * 2048],
                    qT_qb[:, :])
                dma("sync", s_dbg, mdbg[:, (g * nQB + qb) * 8:(g * nQB + qb + 1) * 8],
                    Mrep[:, 0:8])
            sc_free = [[0, 0], [0, 0]]
            pv_read = [[0, 0], [0, 0]]
            exp_done = [[0, 0], [0, 0]]
            wait("tensor", sv, sv.v)
            wait("tensor", sa, sa.v)
            for hp in range(4):
                pvb = (PC, PD) if hp % 2 == 0 else (PE_, PF)
                for jt in range(nJB + 1):
                    if jt < nJB:
                        sl = jt % 2
                        for h2 in range(2):
                            PSs = sc_bank[h2]
                            wait("tensor", sa, sc_free[h2][sl])
                            base = hp * S + jt * 128
                            lo = kT_g[64 * h2:64 * h2 + 64, base:base + 64]
                            hi = kT_g[64 * h2:64 * h2 + 64, base + 64:base + 128]
                            rq = qT_qb[64 * h2:64 * h2 + 64, hp * 512:(hp + 1) * 512]
                            mm(PSs[0:64, sl * 512:(sl + 1) * 512], lo, rq,
                               True, True, tp=(64 * h2, 0))
                            mm(PSs[64:128, sl * 512:(sl + 1) * 512], hi, rq,
                               True, True, tp=(64 * h2, 64), inc=(h2 == 1))
                        p_sc = sp.v
                        for h2 in range(2):
                            wait("scalar", sp, p_sc)
                            wait("scalar", sp, pv_read[h2][sl])
                            hh = 2 * hp + h2
                            e_v = act(pT[h2][sl][:, :],
                                      sc_bank[h2][:, sl * 512:(sl + 1) * 512],
                                      AF.Exp, bias=Mrep[:, hh:hh + 1], scale=SCALE,
                                      inc=True)
                            exp_done[h2][sl] = e_v
                            sc_free[h2][sl] = e_v
                    jp = jt - 1
                    if jp >= 0:
                        slp = jp % 2
                        for h2 in range(2):
                            PSpv = pvb[h2]
                            wait("tensor", sa, exp_done[h2][slp])
                            if jp == 0:
                                wait("tensor", sv, pvb_free[hp % 2][h2])
                            mm(PSpv[0:65, 0:512],
                               v_aug[:, jp * 520 + 65 * (2 * hp + h2):
                                     jp * 520 + 65 * (2 * hp + h2) + 65],
                               pT[h2][slp][:, :],
                               jp == 0, jp == nJB - 1, inc=True)
                            pv_read[h2][slp] = sp.v
                bcb = (PC, PD) if hp % 2 == 1 else (PE_, PF)
                for h2 in range(2):
                    PSpv = pvb[h2]
                    PSbc = bcb[h2]
                    hh = 2 * hp + h2
                    wait("vector", sp, pv_read[h2][(nJB - 1) % 2])
                    v_rc = vop("reciprocal", rec1[0:1, :], PSpv[64:65, 0:512],
                               inc=True)
                    wait("vector", sv, v_rc)
                    v_rb = vop("tensor_copy", rec1b[0:1, :], rec1[0:1, :], inc=True)
                    wait("tensor", sv, v_rb)
                    wait("tensor", sv, bc_free[(hp + 1) % 2][h2])
                    p_bc = mm(PSbc[0:64, 0:512], ones64[0:1, 0:64], rec1b[0:1, :],
                              True, True, inc=True)
                    wait("vector", sp, p_bc)
                    v_eb = vop("tensor_copy", recb[0:64, :], PSbc[0:64, 0:512],
                               inc=True)
                    bc_free[(hp + 1) % 2][h2] = v_eb
                    wait("vector", sv, v_eb)
                    v_no = vop("tensor_mul",
                               attn_oT[64 * h2:64 * h2 + 64, hp * 512:(hp + 1) * 512],
                               PSpv[0:64, 0:512], recb[0:64, :], inc=True)
                    pvb_free[hp % 2][h2] = v_no
            wait("tensor", sv, sv.v)
            if debug:
                wait("sync", sv, sv.v)
                dma("sync", s_dbg, aodbg[:, (g * nQB + qb) * 2048:(g * nQB + qb + 1) * 2048],
                    attn_oT[:, :])
            for qs in range(4):
                bx = qs % 2
                row0 = qb * 512 + qs * 128
                src_rows = x_d[row0:row0 + 128, :] if g == 0 else \
                    acc_hbm[row0:row0 + 128, :]
                vx2 = dma("sync", s_x2[bx], x2[bx][:, :], src_rows)
                for ob in range(2):
                    PSo = wo_psum[ob]
                    if ob == 0:
                        wait("tensor", sa, pe_free_a)
                        wait("tensor", sv, pe_free_v)
                    wait("tensor", sv, wo_free[ob])
                    for c in range(4):
                        mm(PSo[:, 0:512],
                           attn_oT[:, c * 512 + qs * 128: c * 512 + (qs + 1) * 128],
                           Wo_g[:, c * 1024 + ob * 512: c * 1024 + (ob + 1) * 512],
                           c == 0, (g == 1 and c == 3),
                           inc=(g == 1 and c == 3))
                    if g == 0:
                        mm(PSo[:, 0:512], ones1[0:1, :],
                           bo_bf[0:1, ob * 512:(ob + 1) * 512], False, False)
                        mm(PSo[:, 0:512],
                           gatesT_pad[:, row0:row0 + 128],
                           b2pad[:, ob * 512:(ob + 1) * 512], False, True,
                           inc=True)
                    p_wo = sp.v
                    wait("vector", sp, p_wo)
                    wait("vector", s_x2[bx], vx2)
                    wait("vector", s_ac[bx], s_ac[bx].v)
                    v_acc = vop("tensor_add", acc_st[bx][:, ob * 512:(ob + 1) * 512],
                                PSo[:, 0:512], x2[bx][:, ob * 512:(ob + 1) * 512],
                                inc=True)
                    wo_free[ob] = v_acc
                    if ob == 0:
                        pe_free_v = v_acc
                wait("sync", sv, sv.v)
                dma("sync", s_ac[bx], acc_hbm[row0:row0 + 128, :], acc_st[bx][:, :])
            qb_ld_free = sp.v

    flush_phase()
    es_attn.close()

    G = min(4, nTTH)
    NB = 4 * G
    es_p3 = ExitStack()
    xnT_h = sbuf("xnT_h", [128, 8 * (S // 2)], BF16, es_p3)
    yacc = sbuf("yacc", [128, nTTH * 1024], F32, es_p3)
    W1s = [sbuf(f"W1s{i}", [128, 8 * HID], BF16, es_p3) for i in range(2)]
    W2s = [sbuf(f"W2s{i}", [128, 8 * D], BF16, es_p3) for i in range(2)]
    b1e = [sbuf(f"b1e{i}", [1, HID], BF16, es_p3) for i in range(2)]
    hbuf = [sbuf(f"hbuf{i}", [128, HID], BF16, es_p3) for i in range(NB)]
    st6f = sbuf("st6f", [128, 12], F32, es_p3)
    mvAf = sbuf("mvAf", [128, 2], F32, es_p3)
    mvBf = sbuf("mvBf", [128, 2], F32, es_p3)
    msvsf = sbuf("msvsf", [128, 2], F32, es_p3)
    dmtf = sbuf("dmtf", [128, 2], F32, es_p3)
    ms4 = [sbuf(f"ms4_{i}", [128, G], F32, es_p3) for i in range(2)]
    vs4 = [sbuf(f"vs4_{i}", [128, G], F32, es_p3) for i in range(2)]
    dm4 = [sbuf(f"dm4_{i}", [128, G], F32, es_p3) for i in range(2)]
    var4g = [sbuf(f"var4g{i}", [128, G], F32, es_p3) for i in range(2)]
    rstd4 = [sbuf(f"rstd4_{i}", [128, G], F32, es_p3) for i in range(2)]
    nmr4 = [sbuf(f"nmr4_{i}", [128, G], F32, es_p3) for i in range(2)]

    for eng in ENGS:
        for c in (sp, sv, sa):
            wait(eng, c, c.v)
        for i in range(2):
            wait(eng, s_ac[i], s_ac[i].v)

    half_S = S // 2
    SW = S // 2
    nGRP = nTTH // G
    slots = [(hf, e, g2) for hf in range(2) for e in range(NE) for g2 in range(nGRP)]
    NSLOT = len(slots)
    SPE = nGRP

    w1_used = [0, 0]
    w2_used = [0, 0]
    w1_loaded = [0, 0]
    w2_loaded = [0, 0]
    hb_used = [0] * NB
    hpsum_free = [0, 0]
    psT_free = 0
    ob_free = [0, 0]
    norm_done = {}
    gelu_done = {}
    stats_done = {}
    hmm_done = {}
    trans_done = {}
    w2_last = {}
    xh_loaded = [0, 0]
    yi_loaded = [0, 0]
    h_last_of_half = {}

    def load_w1(j):
        if j >= 2 * NE:
            return
        hf, e = divmod(j, NE)
        slot = j % 2
        wait("gpsimd", sp, w1_used[slot])
        dma("gpsimd", s_we[slot],
            W1s[slot][:, :].rearrange("p (c m) -> p c m", c=8),
            W1_d[e].rearrange("(c p) m -> p c m", p=128))
        w1_loaded[slot] = dma("gpsimd", s_we[slot], b1e[slot][0:1, :],
                              b1_d[e:e + 1, :])

    def load_w2(j):
        if j >= 2 * NE:
            return
        hf, e = divmod(j, NE)
        slot = j % 2
        wait("gpsimd", sp, w2_used[slot])
        w2_loaded[slot] = dma("gpsimd", s_w2[slot],
                              W2s[slot][:, :].rearrange("p (c m) -> p c m", c=8),
                              W2_d[e].rearrange("(c p) m -> p c m", p=128))

    v_xh = dma("sync", s_xh, xnT_h[:, :].rearrange("p (c m) -> p c m", c=8),
               xnTh3[:, :, 0:half_S])
    xh_loaded[0] = v_xh
    yi_loaded[0] = dma("sync", s_yi,
                       yacc[:, :].rearrange("p (q d) -> p q d", q=nTTH),
                       acc3[:, 0:nTTH, :])
    w1s_of = {}
    w2s_of = {}
    load_w1(0); load_w2(0)
    w1s_of[0], w2s_of[0] = w1_loaded[0], w2_loaded[0]
    load_w1(1); load_w2(1)
    w1s_of[1], w2s_of[1] = w1_loaded[1], w2_loaded[1]

    def stage_H(k):
        hf, e, g2 = slots[k]
        j = hf * NE + e
        slot = j % 2
        for tg in range(G):
            tt = g2 * G + tg
            bb = (k * G + tg) % 2
            bidx = (k * G + tg) % NB
            PSh = PA if bb == 0 else PB
            if tg == 0 and g2 == 0:
                wait("tensor", s_we[slot], w1s_of[j])
                if e == 0:
                    wait("tensor", s_xh, xh_loaded[hf])
            wait("tensor", sv, hpsum_free[bb])
            wait("tensor", sp, hb_used[bidx])
            for hh2 in range(2):
                for c in range(8):
                    mm(PSh[:, hh2 * 512:(hh2 + 1) * 512],
                       xnT_h[:, c * SW + tt * 128: c * SW + (tt + 1) * 128],
                       W1s[slot][:, c * HID + hh2 * 512: c * HID + (hh2 + 1) * 512],
                       c == 0, False)
                mm(PSh[:, hh2 * 512:(hh2 + 1) * 512], ones1[0:1, :],
                   b1e[slot][0:1, hh2 * 512:(hh2 + 1) * 512], False, True,
                   inc=(hh2 == 1))
            hmm_done[(k, tg)] = sp.v
            wait("vector", sp, hmm_done[(k, tg)])
            vop("bn_stats", st6f[:, 0:6], PSh[:, 0:512])
            vop("bn_stats", st6f[:, 6:12], PSh[:, 512:1024])
            vop("bn_aggr", mvAf[:, :], st6f[:, 0:6])
            v_ag = vop("bn_aggr", mvBf[:, :], st6f[:, 6:12], inc=True)
            wait("vector", sv, v_ag)
            vop("tensor_add", ms4[k % 2][:, tg:tg + 1], mvAf[:, 0:1], mvBf[:, 0:1])
            vop("tensor_add", vs4[k % 2][:, tg:tg + 1], mvAf[:, 1:2], mvBf[:, 1:2])
            vop("tensor_sub", dm4[k % 2][:, tg:tg + 1], mvAf[:, 0:1], mvBf[:, 0:1])
            hpsum_free[bb] = vop("tensor_copy", hbuf[bidx][:, :], PSh[:, 0:1024],
                                 inc=True)
        kr = k % 2
        v_ag = vop("tensor_mul", dm4[kr][:, :], dm4[kr][:, :], dm4[kr][:, :],
                   inc=True)
        wait("vector", sv, v_ag)
        vop("scalar_tensor_tensor", out=var4g[kr][:, :], in0=vs4[kr][:, :],
            scalar=2.0, in1=dm4[kr][:, :], op0=ALU.mult, op1=ALU.add)
        stats_done[k] = sv.v
        if g2 == nGRP - 1 and e == NE - 1:
            h_last_of_half[hf] = sp.v
        w1_used_update = sp.v
        if g2 == nGRP - 1:
            w1_used[(hf * NE + e) % 2] = w1_used_update

    def stage_RSTD_NORM(k):
        kr = k % 2
        wait("scalar", sv, stats_done[k])
        a_ln = act(nmr4[kr][:, :], var4g[kr][:, :], AF.Ln, scale=0.25,
                   bias=epsT[:, 0:1], inc=True)
        wait("scalar", sa, a_ln)
        a_rs = act(rstd4[kr][:, :], nmr4[kr][:, :], AF.Exp, scale=-0.5, inc=True)
        wait("vector", sa, a_rs)
        v_nm = vop("tensor_mul", nmr4[kr][:, :], ms4[kr][:, :], rstd4[kr][:, :],
                   inc=True)
        wait("vector", sv, v_nm)
        v_nm = vop("tensor_scalar", nmr4[kr][:, :], nmr4[kr][:, :], -0.5, None,
                   op0=ALU.mult, inc=True)
        wait("scalar", sv, v_nm)
        for tg in range(G):
            bidx = (k * G + tg) % NB
            act(hbuf[bidx][:, :], hbuf[bidx][:, :], AF.Identity,
                scale=rstd4[kr][:, tg:tg + 1], bias=nmr4[kr][:, tg:tg + 1],
                inc=(tg == G - 1))
        norm_done[k] = sa.v

    def stage_TW(kk):
        k1 = kk - 2
        k2 = kk - 3
        for tg in range(G):
            if 0 <= k1 < NSLOT:
                bidx = (k1 * G + tg) % NB
                if tg == 0:
                    wait("tensor", sa, norm_done[k1])
                wait("tensor", sv, psT_free_ref[0])
                for c in range(8):
                    PSt = PC if c < 4 else PD
                    mm(PSt[:, (c % 4) * 128:((c % 4) + 1) * 128],
                       hbuf[bidx][:, c * 128:(c + 1) * 128], I128[:, :],
                       True, True, inc=(c == 7))
                trans_done[(k1, tg)] = sp.v
                hf1, e1, g21 = slots[k1]
                wait("vector", sp, trans_done[(k1, tg)])
                for c in range(8):
                    PSt = PC if c < 4 else PD
                    vop("tensor_scalar", hbuf[bidx][:, c * 128:(c + 1) * 128],
                        PSt[:, (c % 4) * 128:((c % 4) + 1) * 128],
                        g1T[:, e1 * 8 + c:e1 * 8 + c + 1],
                        be1T[:, e1 * 8 + c:e1 * 8 + c + 1],
                        op0=ALU.mult, op1=ALU.add, inc=(c == 7))
                psT_free_ref[0] = sv.v
                wait("scalar", sv, sv.v)
                gelu_done[(k1, tg)] = act(hbuf[bidx][:, :], hbuf[bidx][:, :],
                                          AF.Gelu, inc=True)
            if 0 <= k2 < NSLOT:
                hf2, e2, g22 = slots[k2]
                j2 = hf2 * NE + e2
                slot2 = j2 % 2
                tt2 = g22 * G + tg
                gtt2 = hf2 * nTTH + tt2
                bidx2 = (k2 * G + tg) % NB
                wait("tensor", sa, gelu_done[(k2, tg)])
                if tg == 0 and g22 == 0:
                    wait("tensor", s_w2[slot2], w2s_of[j2])
                for ob in range(2):
                    PSo = PE_ if ob == 0 else PF
                    wait("tensor", sv, ob_free[ob])
                    for c in range(8):
                        mm(PSo[:, 0:512],
                           hbuf[bidx2][:, c * 128:(c + 1) * 128],
                           W2s[slot2][:, c * D + ob * 512: c * D + (ob + 1) * 512],
                           c == 0, c == 7, inc=(c == 7))
                    p_o = sp.v
                    wait("vector", sp, p_o)
                    if e2 == 0 and g22 == 0 and tg == 0 and ob == 0:
                        wait("vector", s_yi, yi_loaded[hf2])
                    ob_free[ob] = vop(
                        "scalar_tensor_tensor",
                        out=yacc[:, tt2 * 1024 + ob * 512: tt2 * 1024 + (ob + 1) * 512],
                        in0=PSo[:, 0:512],
                        scalar=gates_all[:, NE * gtt2 + e2:NE * gtt2 + e2 + 1],
                        in1=yacc[:, tt2 * 1024 + ob * 512: tt2 * 1024 + (ob + 1) * 512],
                        op0=ALU.mult, op1=ALU.add, inc=True)
                hb_used[bidx2] = sp.v
                if tg == G - 1:
                    w2_last[j2] = sp.v
                    if g22 == nGRP - 1:
                        w2_used[j2 % 2] = sp.v

    psT_free_ref = [0]

    for kk in range(NSLOT + 3):
        if kk < NSLOT:
            stage_H(kk)
            hf, e, g2 = slots[kk]
            j = hf * NE + e
            if g2 == 0:
                if j + 2 < 2 * NE:
                    pass
            if g2 == nGRP - 1:
                if j + 2 < 2 * NE:
                    load_w1(j + 2)
                    w1s_of[j + 2] = w1_loaded[(j + 2) % 2]
                if j == NE - 1:
                    wait("sync", sp, h_last_of_half[0])
                    xh_loaded[1] = dma(
                        "sync", s_xh,
                        xnT_h[:, :].rearrange("p (c m) -> p c m", c=8),
                        xnTh3[:, :, half_S:2 * half_S])
        if 0 <= kk - 1 < NSLOT:
            stage_RSTD_NORM(kk - 1)
        stage_TW(kk)
        if 0 <= kk - 3 < NSLOT:
            hf2, e2, g22 = slots[kk - 3]
            j2 = hf2 * NE + e2
            if g22 == nGRP - 1:
                if j2 + 2 < 2 * NE:
                    load_w2(j2 + 2)
                    w2s_of[j2 + 2] = w2_loaded[(j2 + 2) % 2]
                if e2 == NE - 1:
                    wait("sync", sv, sv.v)
                    dma("sync", s_out,
                        out3[:, hf2 * nTTH:(hf2 + 1) * nTTH, :],
                        yacc[:, :].rearrange("p (q d) -> p q d", q=nTTH))
                    if hf2 == 0:
                        wait("sync", s_out, 16)
                        yi_loaded[1] = dma(
                            "sync", s_yi,
                            yacc[:, :].rearrange("p (q d) -> p q d", q=nTTH),
                            acc3[:, nTTH:2 * nTTH, :])

    wait("sync", s_out, s_out.v)
    flush_phase()
    es_p3.close()
    es_top.close()
    return nc


_NC_CACHE = {}


def kernel(**inp):
    S = 4096
    ncore = 8
    if S not in _NC_CACHE:
        _NC_CACHE[S] = build_nc(S)
    nc = _NC_CACHE[S]

    f32 = np.float32
    x = np.ascontiguousarray(inp["L0_attn_x"], dtype=f32)
    shared = {
        "Wq": np.ascontiguousarray(inp["L0_attn_Wq"], f32),
        "Wl": np.ascontiguousarray(inp["L0_attn_Wl"], f32),
        "Wk": np.ascontiguousarray(inp["L0_attn_Wk"], f32),
        "Wv": np.ascontiguousarray(inp["L0_attn_Wv"], f32),
        "Wo": np.ascontiguousarray(inp["L0_attn_Wo"], f32),
        "Wr": np.ascontiguousarray(inp["L0_ffn_Wr"], f32),
        "bqT": np.ascontiguousarray(inp["L0_attn_bq"].reshape(8, 128).T, f32),
        "bkT": np.ascontiguousarray(inp["L0_attn_bk"].reshape(8, 128).T, f32),
        "blT": np.ascontiguousarray(inp["L0_attn_bl"].reshape(2, 128).T, f32),
        "bv": np.ascontiguousarray(inp["L0_attn_bv"].reshape(1, D), f32),
        "bo": np.ascontiguousarray(inp["L0_attn_bo"].reshape(1, D), f32),
        "br": np.ascontiguousarray(inp["L0_ffn_br"].reshape(1, NE), f32),
        "W1": np.ascontiguousarray(inp["L0_ffn_W1"], f32),
        "b1": np.ascontiguousarray(inp["L0_ffn_b1"], f32),
        "g1T": np.ascontiguousarray(
            inp["L0_ffn_g1"].reshape(NE, 8, 128).transpose(2, 0, 1).reshape(128, 128), f32),
        "be1T": np.ascontiguousarray(
            inp["L0_ffn_be1"].reshape(NE, 8, 128).transpose(2, 0, 1).reshape(128, 128), f32),
        "W2": np.ascontiguousarray(inp["L0_ffn_W2"], f32),
        "b2": np.ascontiguousarray(inp["L0_ffn_b2"], f32),
        "eye": np.eye(128, dtype=f32),
    }
    in_maps = []
    for c in range(ncore):
        m = dict(shared)
        m["x"] = np.ascontiguousarray(x[c])
        in_maps.append(m)
    res = run_bass_kernel_spmd(nc, in_maps, core_ids=list(range(ncore)))
    out = np.stack([np.asarray(r["out"], dtype=np.float32) for r in res.results],
                   axis=0)
    return out


# version 08
# speedup vs baseline: 1.2252x; 1.0575x over previous
import numpy as np
import ml_dtypes
from contextlib import ExitStack

import concourse.bass as bass
import concourse.mybir as mybir
from concourse.bass_utils import run_bass_kernel_spmd

F32 = mybir.dt.float32
BF16 = mybir.dt.bfloat16
AF = mybir.ActivationFunctionType
ALU = mybir.AluOpType
AX = mybir.AxisListType.X

D = 1024
H = 16
DH = 64
LAT = 256
NE = 16
HID = 1024
SCALE = DH ** -0.5
EPS = 1e-5


class Ctr:
    def __init__(self, nc, es, name):
        self.h = es.enter_context(nc.semaphore(name))
        self.v = 0

    def bump(self, n=1):
        self.v += n
        return self.v


def build_nc(S=4096, debug=False):
    nT = S // 128
    nQB = S // 512
    nJB = S // 128
    nG = S // 512
    nTTH = nT // 2

    nc = bass.Bass("TRN2", target_bir_lowering=False)

    x_d = nc.dram_tensor("x", [S, D], F32, kind="ExternalInput").ap()
    Wq_d = nc.dram_tensor("Wq", [D, D], F32, kind="ExternalInput").ap()
    Wl_d = nc.dram_tensor("Wl", [D, LAT], F32, kind="ExternalInput").ap()
    Wk_d = nc.dram_tensor("Wk", [LAT, D], F32, kind="ExternalInput").ap()
    Wv_d = nc.dram_tensor("Wv", [LAT, D], F32, kind="ExternalInput").ap()
    Wo_d = nc.dram_tensor("Wo", [D, D], F32, kind="ExternalInput").ap()
    Wr_d = nc.dram_tensor("Wr", [D, NE], F32, kind="ExternalInput").ap()
    bqT_d = nc.dram_tensor("bqT", [128, 8], F32, kind="ExternalInput").ap()
    bkT_d = nc.dram_tensor("bkT", [128, 8], F32, kind="ExternalInput").ap()
    blT_d = nc.dram_tensor("blT", [128, 2], F32, kind="ExternalInput").ap()
    bv_d = nc.dram_tensor("bv", [1, D], F32, kind="ExternalInput").ap()
    bo_d = nc.dram_tensor("bo", [1, D], F32, kind="ExternalInput").ap()
    br_d = nc.dram_tensor("br", [1, NE], F32, kind="ExternalInput").ap()
    W1_d = nc.dram_tensor("W1", [NE, D, HID], F32, kind="ExternalInput").ap()
    b1_d = nc.dram_tensor("b1", [NE, HID], F32, kind="ExternalInput").ap()
    g1T_d = nc.dram_tensor("g1T", [128, 128], F32, kind="ExternalInput").ap()
    be1T_d = nc.dram_tensor("be1T", [128, 128], F32, kind="ExternalInput").ap()
    W2_d = nc.dram_tensor("W2", [NE, HID, D], F32, kind="ExternalInput").ap()
    b2_d = nc.dram_tensor("b2", [NE, D], F32, kind="ExternalInput").ap()
    eye_d = nc.dram_tensor("eye", [128, 128], F32, kind="ExternalInput").ap()
    out_d = nc.dram_tensor("out", [S, D], F32, kind="ExternalOutput").ap()
    ikind = "ExternalOutput" if debug else "Internal"
    xnT_hbm = nc.dram_tensor("xnT_hbm", [128, 8 * S], BF16, kind=ikind).ap()
    acc_hbm = nc.dram_tensor("acc_hbm", [S, D], F32, kind=ikind).ap()
    gdbg = nc.dram_tensor("gdbg", [128, NE * (S // 128)], F32,
                          kind="ExternalOutput").ap() if debug else None
    if debug:
        kdbg = nc.dram_tensor("kdbg", [128, 8 * S], BF16, kind="ExternalOutput").ap()
        qdbg = nc.dram_tensor("qdbg", [128, 2 * (S // 512) * 2048], BF16,
                              kind="ExternalOutput").ap()
        vdbg = nc.dram_tensor("vdbg", [128, 2 * (S // 128) * 520], BF16,
                              kind="ExternalOutput").ap()
        aodbg = nc.dram_tensor("aodbg", [128, 2 * (S // 512) * 2048], BF16,
                               kind="ExternalOutput").ap()
        mdbg = nc.dram_tensor("mdbg", [128, 2 * (S // 512) * 8], F32,
                              kind="ExternalOutput").ap()

    xnTh3 = xnT_hbm.rearrange("p (c s) -> p c s", c=8)
    acc3 = acc_hbm.rearrange("(q p) d -> p q d", p=128)
    out3 = out_d.rearrange("(q p) d -> p q d", p=128)

    ENGS = ("sync", "gpsimd", "vector", "scalar", "tensor")
    sched = {e: [] for e in ENGS}

    def em(eng, fn):
        sched[eng].append(fn)

    def flush_phase():
        with nc.Block() as blk:
            @blk.sync
            def _(e):
                for f in sched["sync"]:
                    f(e)

            @blk.gpsimd
            def _(e):
                for f in sched["gpsimd"]:
                    f(e)

            @blk.vector
            def _(e):
                for f in sched["vector"]:
                    f(e)

            @blk.scalar
            def _(e):
                for f in sched["scalar"]:
                    f(e)

            @blk.tensor
            def _(e):
                for f in sched["tensor"]:
                    f(e)
        for e in ENGS:
            sched[e].clear()

    es_top = ExitStack()

    def sbuf(name, shape, dt, stack):
        return stack.enter_context(nc.sbuf_tensor(name, shape, dt))

    sp = Ctr(nc, es_top, "sp")
    sv = Ctr(nc, es_top, "sv")
    sa = Ctr(nc, es_top, "sa")
    s_wA = Ctr(nc, es_top, "s_wA")
    s_bias = Ctr(nc, es_top, "s_bias")
    s_xt = [Ctr(nc, es_top, f"s_xt{i}") for i in range(2)]
    s_spl = [Ctr(nc, es_top, f"s_spl{i}") for i in range(2)]
    s_wg = Ctr(nc, es_top, "s_wg")
    s_qb = [Ctr(nc, es_top, f"s_qb{i}") for i in range(2)]
    s_x2 = [Ctr(nc, es_top, f"s_x2{i}") for i in range(2)]
    s_ac = [Ctr(nc, es_top, f"s_ac{i}") for i in range(2)]
    s_we = [Ctr(nc, es_top, f"s_we{i}") for i in range(2)]
    s_w2 = [Ctr(nc, es_top, f"s_w2{i}") for i in range(2)]
    s_xh = Ctr(nc, es_top, "s_xh")
    s_yi = Ctr(nc, es_top, "s_yi")
    s_out = Ctr(nc, es_top, "s_out")
    s_eye = Ctr(nc, es_top, "s_eye")
    s_dbg = Ctr(nc, es_top, "s_dbg") if debug else None

    def wait(eng, ctr, val):
        if val > 0:
            em(eng, lambda e, c=ctr.h, v=val: e.wait_ge(c, v))

    def mm(out_ap, lhsT, rhs, start, stop, inc=False, tp=None):
        if inc:
            val = sp.bump()
            em("tensor", lambda e, o=out_ap, l=lhsT, r=rhs, s1=start, s2=stop, t=tp:
               e.matmul(o, l, r, start=s1, stop=s2, tile_position=t).then_inc(sp.h, 1))
            return val
        em("tensor", lambda e, o=out_ap, l=lhsT, r=rhs, s1=start, s2=stop, t=tp:
           e.matmul(o, l, r, start=s1, stop=s2, tile_position=t))
        return None

    def act(out_ap, in_ap, func, bias=0.0, scale=1.0, inc=False):
        if inc:
            val = sa.bump()
            em("scalar", lambda e, o=out_ap, i=in_ap, f=func, b=bias, s=scale:
               e.activation(o, i, f, bias=b, scale=s).then_inc(sa.h, 1))
            return val
        em("scalar", lambda e, o=out_ap, i=in_ap, f=func, b=bias, s=scale:
           e.activation(o, i, f, bias=b, scale=s))
        return None

    def vop(name, *args, inc=False, **kw):
        if inc:
            val = sv.bump()
            em("vector", lambda e, n=name, a=args, k=kw:
               getattr(e, n)(*a, **k).then_inc(sv.h, 1))
            return val
        em("vector", lambda e, n=name, a=args, k=kw: getattr(e, n)(*a, **k))
        return None

    def dma(eng, ctr, out_ap, in_ap):
        val = ctr.bump(16)
        em(eng, lambda e, o=out_ap, i=in_ap, c=ctr.h:
           e.dma_start(o, i).then_inc(c, 16))
        return val

    ones1 = sbuf("ones1", [1, 128], BF16, es_top)
    ones64 = sbuf("ones64", [1, 64], BF16, es_top)
    ind_lo = sbuf("ind_lo", [128, 1], BF16, es_top)
    ind_hi = sbuf("ind_hi", [128, 1], BF16, es_top)
    I128 = sbuf("I128", [128, 128], BF16, es_top)
    epsT = sbuf("epsT", [128, 1], F32, es_top)
    lnS = sbuf("lnS", [1, 1], F32, es_top)
    g1T = sbuf("g1Ts", [128, 128], F32, es_top)
    be1T = sbuf("be1Ts", [128, 128], F32, es_top)
    gates_all = sbuf("gates_all", [128, NE * nT], F32, es_top)

    PA = es_top.enter_context(nc.psum_tensor("PA", [128, 1024], F32))
    PB = es_top.enter_context(nc.psum_tensor("PB", [128, 1024], F32))
    PC = es_top.enter_context(nc.psum_tensor("PC", [128, 512], F32))
    PD = es_top.enter_context(nc.psum_tensor("PD", [128, 512], F32))
    PE_ = es_top.enter_context(nc.psum_tensor("PE", [128, 512], F32))
    PF = es_top.enter_context(nc.psum_tensor("PF", [128, 512], F32))
    PA3 = PA[:, :].rearrange("p (c t) -> p c t", c=8)
    PB3 = PB[:, :].rearrange("p (c t) -> p c t", c=8)

    dma("gpsimd", s_eye, I128[:, :], eye_d)
    for dst, src in ((g1T[:, :], g1T_d), (be1T[:, :], be1T_d)):
        dma("sync", s_bias, dst, src)
    vop("memset", ones1[:, :], 1.0)
    vop("memset", ones64[:, :], 1.0)
    vop("memset", ind_lo[0:64, :], 1.0)
    vop("memset", ind_lo[64:128, :], 0.0)
    vop("memset", ind_hi[0:64, :], 0.0)
    vop("memset", ind_hi[64:128, :], 1.0)
    vop("memset", epsT[:, :], EPS)
    consts_done = vop("memset", lnS[:, :], float(np.log(SCALE)), inc=True)

    es_attn = ExitStack()
    bqT = sbuf("bqTs", [128, 8], F32, es_attn)
    bkT = sbuf("bkTs", [128, 8], F32, es_attn)
    blT = sbuf("blTs", [128, 2], F32, es_attn)
    bv_bf = sbuf("bv_bf", [1, D], BF16, es_attn)
    bo_bf = sbuf("bo_bf", [1, D], BF16, es_attn)
    br_bf = sbuf("br_bf", [1, NE], BF16, es_attn)
    b2pad = sbuf("b2pad", [128, D], BF16, es_attn)
    gatesT_pad = sbuf("gatesT_pad", [128, S], BF16, es_attn)
    Wr_s = sbuf("Wr_s", [128, 8 * NE], BF16, es_attn)
    latT = sbuf("latT", [128, 2 * S], BF16, es_attn)
    for dst, srcp in (
        (bv_bf[0:1, :], bv_d),
        (bo_bf[0:1, :], bo_d),
        (br_bf[0:1, :], br_d),
        (Wr_s[:, :].rearrange("p (c m) -> p c m", c=8),
         Wr_d.rearrange("(c p) m -> p c m", p=128)),
    ):
        dma("gpsimd", s_wA, dst, srcp)
    for dst, srcp in ((bqT[:, :], bqT_d), (bkT[:, :], bkT_d), (blT[:, :], blT_d)):
        dma("sync", s_bias, dst, srcp)
    vop("memset", gatesT_pad[:, :], 0.0)
    consts2_done = vop("memset", b2pad[:, :], 0.0, inc=True)
    wait("gpsimd", sv, consts2_done)
    dma("gpsimd", s_wA, b2pad[0:16, :], b2_d)
    kT_g = sbuf("kT_g", [128, 4 * S], BF16, es_attn)
    v_aug = sbuf("v_aug", [128, nJB * 520], BF16, es_attn)
    Wl_s = sbuf("Wl_s", [128, 8 * LAT], BF16, es_attn)
    Wq_g = sbuf("Wq_g", [128, 8 * 512], BF16, es_attn)
    Wo_g = sbuf("Wo_g", [128, 4 * 1024], BF16, es_attn)
    Wk_g = sbuf("Wk_g", [128, 2 * 512], BF16, es_attn)
    Wv_g = sbuf("Wv_g", [128, 2 * 512], BF16, es_attn)
    xnT_qb = sbuf("xnT_qb", [128, 8 * 512], BF16, es_attn)
    qT_qb = sbuf("qT_qb", [128, 4 * 512], BF16, es_attn)
    pT = [[sbuf(f"pT{h}_{i}", [128, 512], BF16, es_attn) for i in range(2)]
          for h in range(2)]
    attn_oT = sbuf("attn_oT", [128, 4 * 512], BF16, es_attn)
    sqs = [sbuf(f"sqs{i}", [128, 512], BF16, es_attn) for i in range(2)]
    krow8a = sbuf("krow8a", [1, 8], F32, es_attn)
    krow8b = sbuf("krow8b", [1, 8], F32, es_attn)
    kn1 = sbuf("kn1", [1, 8], F32, es_attn)
    qn1 = sbuf("qn1", [1, 8], F32, es_attn)
    t8 = sbuf("t8", [1, 8], F32, es_attn)
    l8 = sbuf("l8", [1, 8], F32, es_attn)
    m8 = sbuf("m8", [1, 8], F32, es_attn)
    negm8 = sbuf("negm8", [1, 8], F32, es_attn)
    negm8b = sbuf("negm8b", [1, 8], BF16, es_attn)
    Mrep = sbuf("Mrep", [128, 8], F32, es_attn)
    rec1 = sbuf("rec1", [1, 512], F32, es_attn)
    rec1b = sbuf("rec1b", [1, 512], BF16, es_attn)
    recb = sbuf("recb", [64, 512], F32, es_attn)
    x2 = [sbuf(f"x2_{i}", [128, D], F32, es_attn) for i in range(2)]
    acc_st = [sbuf(f"acc_st{i}", [128, D], F32, es_attn) for i in range(2)]
    v_aug3 = v_aug[:, :].rearrange("p (j h dd) -> p j h dd", h=8, dd=65)

    es_p1 = ExitStack()
    xt = [sbuf(f"xt{i}", [128, D], F32, es_p1) for i in range(2)]
    xnbf = [sbuf(f"xnbf{i}", [128, D], BF16, es_p1) for i in range(2)]
    xnW = [sbuf(f"xnW{i}", [128, 8 * 512], BF16, es_p1) for i in range(2)]
    st6 = sbuf("st6", [128, 12], F32, es_p1)
    mvA = sbuf("mvA", [128, 2], F32, es_p1)
    mvB = sbuf("mvB", [128, 2], F32, es_p1)
    msvs = sbuf("msvs", [128, 2], F32, es_p1)
    dmt = sbuf("dmt", [128, 2], F32, es_p1)
    var4 = sbuf("var4", [128, 1], F32, es_p1)
    t1 = sbuf("t1", [128, 1], F32, es_p1)
    rstd = sbuf("rstd", [128, 1], F32, es_p1)
    nmr = sbuf("nmr", [128, 1], F32, es_p1)
    negm1 = sbuf("negm1", [128, 1], F32, es_p1)
    exps = sbuf("exps", [128, NE], F32, es_p1)
    mx8 = sbuf("mx8", [128, 8], F32, es_p1)
    den = sbuf("den", [128, 1], F32, es_p1)
    rden = sbuf("rden", [128, 1], F32, es_p1)
    maskg = sbuf("maskg", [128, NE], F32, es_p1)
    gp_t = sbuf("gp_t", [128, NE], F32, es_p1)
    gbf = sbuf("gbf", [128, NE], BF16, es_p1)
    xnW3 = [w[:, :].rearrange("p (c t) -> p c t", c=8) for w in xnW]

    dma("gpsimd", s_wA, Wl_s[:, :].rearrange("p (c m) -> p c m", c=8),
        Wl_d.rearrange("(c p) m -> p c m", p=128))
    wA_all = s_wA.v

    xt_free = [0, 0]
    tr_bank_free = [0, 0]
    ps6_free_a, ps6_free_v = 0, 0
    ps7_free = 0
    lat_bank_free = [0, 0]
    spill_done = []

    wait("tensor", s_eye, s_eye.v)
    wait("tensor", sv, consts_done)
    first_router = [True]
    first_lat = [True]

    for tt in range(nT):
        b = tt % 2
        gi = tt // 4
        tg = tt % 4
        wsl = gi % 2
        wait("sync", sa, xt_free[b])
        vx = dma("sync", s_xt[b], xt[b][:, :], x_d[tt * 128:(tt + 1) * 128, :])
        wait("vector", s_xt[b], vx)
        vop("bn_stats", st6[:, 0:6], xt[b][:, 0:512])
        vop("bn_stats", st6[:, 6:12], xt[b][:, 512:1024])
        vop("bn_aggr", mvA[:, :], st6[:, 0:6])
        v_ag = vop("bn_aggr", mvB[:, :], st6[:, 6:12], inc=True)
        wait("vector", sv, v_ag)
        vop("tensor_add", msvs[:, :], mvA[:, :], mvB[:, :])
        v_ag = vop("tensor_sub", dmt[:, :], mvA[:, :], mvB[:, :], inc=True)
        wait("vector", sv, v_ag)
        v_ag = vop("tensor_mul", dmt[:, :], dmt[:, :], dmt[:, :], inc=True)
        wait("vector", sv, v_ag)
        v_stats = vop("scalar_tensor_tensor", out=var4[:, :], in0=msvs[:, 1:2],
                      scalar=2.0, in1=dmt[:, 0:1], op0=ALU.mult, op1=ALU.add,
                      inc=True)
        wait("scalar", sv, v_stats)
        a_ln = act(t1[:, :], var4[:, :], AF.Ln, scale=0.25, bias=epsT[:, 0:1],
                   inc=True)
        wait("scalar", sa, a_ln)
        a_rstd = act(rstd[:, :], t1[:, :], AF.Exp, scale=-0.5, inc=True)
        wait("vector", sa, a_rstd)
        v_nmr = vop("tensor_scalar", nmr[:, :], msvs[:, 0:1], rstd[:, 0:1], -0.5,
                    op0=ALU.mult, op1=ALU.mult, inc=True)
        wait("scalar", sv, v_nmr)
        a_xn = act(xnbf[b][:, :], xt[b][:, :], AF.Identity, scale=rstd[:, 0:1],
                   bias=nmr[:, 0:1], inc=True)
        xt_free[b] = a_xn
        PS, PS3 = (PA, PA3) if b == 0 else (PB, PB3)
        wait("tensor", sa, a_xn)
        wait("tensor", sv, tr_bank_free[b])
        for c in range(8):
            mm(PS[:, c * 128:(c + 1) * 128], xnbf[b][:, c * 128:(c + 1) * 128],
               I128[:, :], True, True, inc=(c == 7))
        p_tr = sp.v
        wait("vector", sp, p_tr)
        if tg == 0 and gi >= 2:
            wait("vector", s_spl[wsl], spill_done[gi - 2])
        v_ev = vop("tensor_copy", xnW3[wsl][:, :, tg * 128:(tg + 1) * 128], PS3,
                   inc=True)
        tr_bank_free[b] = v_ev
        wait("tensor", sv, v_ev)
        if first_router[0]:
            wait("tensor", s_wA, wA_all)
            first_router[0] = False
        wait("tensor", sa, ps6_free_a)
        wait("tensor", sv, ps6_free_v)
        for c in range(8):
            mm(PE_[:, 0:NE], xnW[wsl][:, c * 512 + tg * 128: c * 512 + (tg + 1) * 128],
               Wr_s[:, c * NE:(c + 1) * NE], c == 0, False)
        p_rt = mm(PE_[:, 0:NE], ones1[0:1, :], br_bf[0:1, :], False, True, inc=True)
        wait("vector", sp, p_rt)
        v_nm1 = vop("reduce_max", negm1[:, 0:1], PE_[:, 0:NE], axis=AX, negate=True,
                    inc=True)
        ps6_free_v = v_nm1
        wait("scalar", sv, v_nm1)
        a_ex = act(exps[:, :], PE_[:, 0:NE], AF.Exp, bias=negm1[:, 0:1], inc=True)
        ps6_free_a = a_ex
        wait("vector", sa, a_ex)
        v_mx = vop("max", out=mx8[:, :], in_=exps[:, :], inc=True)
        wait("vector", sv, v_mx)
        vop("tensor_scalar", maskg[:, :], exps[:, :], mx8[:, 1:2], None,
            op0=ALU.is_ge)
        v_e1 = vop("tensor_add", den[:, :], mx8[:, 0:1], mx8[:, 1:2], inc=True)
        wait("vector", sv, v_e1)
        vop("tensor_mul", gp_t[:, :], exps[:, :], maskg[:, :])
        v_rd = vop("reciprocal", rden[:, :], den[:, :], inc=True)
        wait("vector", sv, v_rd)
        v_e2 = vop("tensor_scalar", gates_all[:, NE * tt:NE * (tt + 1)], gp_t[:, :],
                   rden[:, 0:1], None, op0=ALU.mult, inc=True)
        wait("vector", sv, v_e2)
        v_gb = vop("tensor_copy", gbf[:, :], gates_all[:, NE * tt:NE * (tt + 1)],
                   inc=True)
        wait("tensor", sv, v_gb)
        wait("tensor", sv, ps7_free)
        p_gt = mm(PF[0:NE, 0:128], gbf[:, :], I128[:, :], True, True, inc=True)
        wait("vector", sp, p_gt)
        ps7_free = vop("tensor_copy", gatesT_pad[0:NE, tt * 128:(tt + 1) * 128],
                       PF[0:NE, 0:128], inc=True)

        if tg == 3:
            for li in range(2):
                PSl = PC if li == 0 else PD
                wait("tensor", sa, lat_bank_free[li])
                for c in range(8):
                    mm(PSl[:, 0:512],
                       Wl_s[:, c * LAT + 128 * li: c * LAT + 128 * li + 128],
                       xnW[wsl][:, c * 512: c * 512 + 512],
                       c == 0, c == 7, inc=(c == 7))
                p_l = sp.v
                wait("scalar", sp, p_l)
                if first_lat[0]:
                    wait("scalar", s_bias, s_bias.v)
                    first_lat[0] = False
                lat_bank_free[li] = act(
                    latT[:, li * S + gi * 512: li * S + (gi + 1) * 512],
                    PSl[:, 0:512], AF.Identity, scale=1.0,
                    bias=blT[:, li:li + 1], inc=True)
            wait("sync", sv, sv.v)
            spill_done.append(
                dma("sync", s_spl[wsl], xnTh3[:, :, gi * 512:(gi + 1) * 512],
                    xnW3[wsl]))

    if debug:
        wait("sync", sv, sv.v)
        dma("sync", s_bias, gdbg, gates_all[:, :])
    flush_phase()
    es_p1.close()

    for eng in ENGS:
        for c in (sp, sv, sa):
            wait(eng, c, c.v)
        for i in range(2):
            wait(eng, s_spl[i], s_spl[i].v)

    pv_bank = {0: PC, 1: PD}
    sc_bank = {0: PA, 1: PB}

    wo_psum = {0: PE_, 1: PF}

    acc_written = 0

    for g in range(2):
        if g == 1:
            for eng in ENGS:
                for c in (sp, sv, sa):
                    wait(eng, c, c.v)
                for i in range(2):
                    wait(eng, s_ac[i], s_ac[i].v)
        wv0 = dma("gpsimd", s_wg,
                  Wq_g[:, :].rearrange("p (c m) -> p c m", c=8),
                  Wq_d.rearrange("(c p) m -> p c m", p=128)[:, :, 512 * g:512 * (g + 1)])
        dma("gpsimd", s_wg,
            Wo_g[:, :].rearrange("p (c m) -> p c m", c=4),
            Wo_d.rearrange("(c p) m -> p c m", p=128)[:, 4 * g:4 * (g + 1), :])
        dma("gpsimd", s_wg,
            Wk_g[:, :].rearrange("p (c m) -> p c m", c=2),
            Wk_d.rearrange("(c p) m -> p c m", p=128)[:, :, 512 * g:512 * (g + 1)])
        wv3 = dma("gpsimd", s_wg,
                  Wv_g[:, :].rearrange("p (c m) -> p c m", c=2),
                  Wv_d.rearrange("(c p) m -> p c m", p=128)[:, :, 512 * g:512 * (g + 1)])

        wait("tensor", s_wg, wv3)
        kb_free = [0, 0]
        for hp in range(4):
            for jc in range(S // 512):
                bb = (hp * (S // 512) + jc) % 2
                PSl = PC if bb == 0 else PD
                wait("tensor", sa, kb_free[bb])
                for li in range(2):
                    mm(PSl[:, 0:512],
                       Wk_g[:, li * 512 + hp * 128: li * 512 + (hp + 1) * 128],
                       latT[:, li * S + jc * 512: li * S + (jc + 1) * 512],
                       li == 0, li == 1, inc=(li == 1))
                wait("scalar", sp, sp.v)
                kb_free[bb] = act(
                    kT_g[:, hp * S + jc * 512: hp * S + (jc + 1) * 512],
                    PSl[:, 0:512], AF.Identity, scale=1.0,
                    bias=bkT[:, 4 * g + hp:4 * g + hp + 1], inc=True)

        if debug:
            wait("sync", sa, sa.v)
            dma("sync", s_dbg, kdbg[:, g * 4 * S:(g + 1) * 4 * S], kT_g[:, :])
        v_ones = vop("memset", v_aug[:, :], 1.0, inc=True)
        wait("tensor", sv, v_ones)
        vb_free = [0, 0]
        for jt in range(nJB):
            bb = jt % 2
            PSl = PC if bb == 0 else PD
            wait("tensor", sv, vb_free[bb])
            wait("tensor", sa, kb_free[bb])
            for li in range(2):
                mm(PSl[:, 0:512],
                   latT[:, li * S + jt * 128: li * S + (jt + 1) * 128],
                   Wv_g[:, li * 512: (li + 1) * 512],
                   li == 0, False)
            mm(PSl[:, 0:512], ones1[0:1, :], bv_bf[0:1, 512 * g:512 * (g + 1)],
               False, True, inc=True)
            wait("vector", sp, sp.v)
            vb_free[bb] = vop(
                "tensor_copy", v_aug3[:, jt, :, 0:64],
                PSl[:, 0:512].rearrange("p (h dd) -> p h dd", h=8), inc=True)

        if debug:
            wait("sync", sv, sv.v)
            dma("sync", s_dbg, vdbg[:, g * (S // 128) * 520:(g + 1) * (S // 128) * 520],
                v_aug[:, :])
        pe_free_a = 0
        pe_free_v = 0
        nm_free = [0, 0]
        sq_done = {}
        nJC = S // 512
        for hp in range(4):
            for jc in range(nJC):
                sq_done[jc] = vop(
                    "tensor_mul", sqs[jc % 2][:, :],
                    kT_g[:, hp * S + jc * 512: hp * S + (jc + 1) * 512],
                    kT_g[:, hp * S + jc * 512: hp * S + (jc + 1) * 512],
                    inc=(True))
                if jc >= 1:
                    jp = jc - 1
                    for h2 in range(2):
                        ind = ind_lo if h2 == 0 else ind_hi
                        kr8 = krow8a if h2 == 0 else krow8b
                        PSn = PC if h2 == 0 else PD
                        wait("tensor", sv, sq_done[jp])
                        wait("tensor", sv, nm_free[h2])
                        p_n = mm(PSn[0:1, 0:512], ind[:, 0:1], sqs[jp % 2][:, :],
                                 True, True, inc=True)
                        wait("vector", sp, p_n)
                        nm_free[h2] = vop("reduce_max", kr8[0:1, jp:jp + 1],
                                          PSn[0:1, 0:512], axis=AX, inc=True)
            jp = nJC - 1
            for h2 in range(2):
                ind = ind_lo if h2 == 0 else ind_hi
                kr8 = krow8a if h2 == 0 else krow8b
                PSn = PC if h2 == 0 else PD
                wait("tensor", sv, sq_done[jp])
                wait("tensor", sv, nm_free[h2])
                p_n = mm(PSn[0:1, 0:512], ind[:, 0:1], sqs[jp % 2][:, :],
                         True, True, inc=True)
                wait("vector", sp, p_n)
                nm_free[h2] = vop("reduce_max", kr8[0:1, jp:jp + 1],
                                  PSn[0:1, 0:512], axis=AX, inc=True)
            pe_free_v = max(nm_free)
            wait("vector", sv, pe_free_v)
            vop("reduce_max", kn1[0:1, 2 * hp:2 * hp + 1], krow8a[0:1, 0:nJC],
                axis=AX)
            vop("reduce_max", kn1[0:1, 2 * hp + 1:2 * hp + 2], krow8b[0:1, 0:nJC],
                axis=AX)
        v_kn = vop("tensor_copy", t8[0:1, 0:8], kn1[0:1, 0:8], inc=True)
        wait("vector", sv, v_kn)
        knorm_free = [nm_free[0], nm_free[1]]

        wait("tensor", sv, vb_free[0])
        wait("tensor", sv, vb_free[1])
        qb_ld_free = 0
        q_first = True
        wo_free = [0, 0]
        pvb_free = [[0, 0], [0, 0]]
        bc_free = [[0, 0], [0, 0]]
        for qb in range(nQB):
            qs_sem = s_qb[qb % 2]
            wait("sync", sp, qb_ld_free)
            vq = dma("sync", qs_sem, xnT_qb[:, :].rearrange("p (c m) -> p c m", c=8),
                     xnTh3[:, :, qb * 512:(qb + 1) * 512])
            wait("tensor", qs_sem, vq)
            if q_first:
                wait("tensor", s_wg, wv0)
                q_first = False
            qp_done = [0, 0, 0, 0]
            for hp in range(4):
                PSq = PC if hp % 2 == 0 else PD
                if qb == 0:
                    wait("tensor", sv, knorm_free[hp % 2])
                wait("tensor", sv, pvb_free[0][hp % 2])
                wait("tensor", sv, bc_free[0][hp % 2])
                if hp >= 2:
                    wait("tensor", sa, qp_done[hp - 2])
                for c in range(8):
                    mm(PSq[:, 0:512],
                       Wq_g[:, c * 512 + hp * 128: c * 512 + (hp + 1) * 128],
                       xnT_qb[:, c * 512:(c + 1) * 512],
                       c == 0, c == 7, inc=(c == 7))
                wait("scalar", sp, sp.v)
                qp_done[hp] = act(qT_qb[:, hp * 512:(hp + 1) * 512],
                                  PSq[:, 0:512], AF.Identity, scale=1.0,
                                  bias=bqT[:, 4 * g + hp:4 * g + hp + 1],
                                  inc=True)
            pe_free_a = qp_done[3]
            nm_free = [0, 0]
            sqq_done = [0, 0, 0, 0]
            sq_read = [0, 0, 0, 0]
            for hp in range(4):
                wait("vector", sa, qp_done[hp])
                if hp >= 2:
                    wait("vector", sp, sq_read[hp - 2])
                sqq_done[hp] = vop("tensor_mul", sqs[hp % 2][:, :],
                                   qT_qb[:, hp * 512:(hp + 1) * 512],
                                   qT_qb[:, hp * 512:(hp + 1) * 512], inc=True)
            for hp in range(4):
                for h2 in range(2):
                    hh = hp * 2 + h2
                    ind = ind_lo if h2 == 0 else ind_hi
                    PSn = PE_ if hh % 2 == 0 else PF
                    wait("tensor", sv, sqq_done[hp])
                    if hh < 2:
                        wait("tensor", sv, pvb_free[1][hh % 2])
                        wait("tensor", sv, bc_free[1][hh % 2])
                        wait("tensor", sv, wo_free[hh % 2])
                    wait("tensor", sv, nm_free[hh % 2])
                    p_n = mm(PSn[0:1, 0:512], ind[:, 0:1], sqs[hp % 2][:, :],
                             True, True, inc=True)
                    wait("vector", sp, p_n)
                    sq_read[hp] = sp.v
                    nm_free[hh % 2] = vop("reduce_max", qn1[0:1, hh:hh + 1],
                                          PSn[0:1, 0:512], axis=AX, inc=True)
            pe_free_v = max(nm_free)
            wait("vector", sv, pe_free_v)
            v_t8 = vop("tensor_mul", t8[0:1, 0:8], qn1[0:1, 0:8], kn1[0:1, 0:8],
                       inc=True)
            wait("scalar", sv, v_t8)
            a_l8 = act(l8[0:1, 0:8], t8[0:1, 0:8], AF.Ln, inc=True)
            wait("scalar", sa, a_l8)
            a_m8 = act(m8[0:1, 0:8], l8[0:1, 0:8], AF.Exp, scale=0.5,
                       bias=lnS[0:1, 0:1], inc=True)
            wait("vector", sa, a_m8)
            v_e3 = vop("tensor_scalar", negm8[0:1, 0:8], m8[0:1, 0:8], -1.0, None,
                       op0=ALU.mult, inc=True)
            wait("vector", sv, v_e3)
            v_m8 = vop("tensor_copy", negm8b[0:1, 0:8], negm8[0:1, 0:8], inc=True)
            wait("tensor", sv, v_m8)
            wait("tensor", sv, pe_free_v)
            wait("tensor", sa, pe_free_a)
            p_mb = mm(PE_[:, 0:8], ones1[0:1, :], negm8b[0:1, 0:8], True, True,
                      inc=True)
            wait("vector", sp, p_mb)
            v_mr = vop("tensor_copy", Mrep[:, 0:8], PE_[:, 0:8], inc=True)
            pe_free_v = v_mr
            wait("scalar", sv, v_mr)

            if debug:
                wait("sync", sa, sa.v)
                wait("sync", sv, sv.v)
                dma("sync", s_dbg, qdbg[:, (g * nQB + qb) * 2048:(g * nQB + qb + 1) * 2048],
                    qT_qb[:, :])
                dma("sync", s_dbg, mdbg[:, (g * nQB + qb) * 8:(g * nQB + qb + 1) * 8],
                    Mrep[:, 0:8])
            sc_free = [[0, 0], [0, 0]]
            pv_read = [[0, 0], [0, 0]]
            exp_done = [[0, 0], [0, 0]]
            wait("tensor", sv, sv.v)
            wait("tensor", sa, sa.v)
            for hp in range(4):
                pvb = (PC, PD) if hp % 2 == 0 else (PE_, PF)
                for jt in range(nJB + 1):
                    if jt < nJB:
                        sl = jt % 2
                        for h2 in range(2):
                            PSs = sc_bank[h2]
                            wait("tensor", sa, sc_free[h2][sl])
                            base = hp * S + jt * 128
                            lo = kT_g[64 * h2:64 * h2 + 64, base:base + 64]
                            hi = kT_g[64 * h2:64 * h2 + 64, base + 64:base + 128]
                            rq = qT_qb[64 * h2:64 * h2 + 64, hp * 512:(hp + 1) * 512]
                            mm(PSs[0:64, sl * 512:(sl + 1) * 512], lo, rq,
                               True, True, tp=(64 * h2, 0))
                            mm(PSs[64:128, sl * 512:(sl + 1) * 512], hi, rq,
                               True, True, tp=(64 * h2, 64), inc=(h2 == 1))
                        p_sc = sp.v
                        for h2 in range(2):
                            wait("scalar", sp, p_sc)
                            wait("scalar", sp, pv_read[h2][sl])
                            hh = 2 * hp + h2
                            e_v = act(pT[h2][sl][:, :],
                                      sc_bank[h2][:, sl * 512:(sl + 1) * 512],
                                      AF.Exp, bias=Mrep[:, hh:hh + 1], scale=SCALE,
                                      inc=True)
                            exp_done[h2][sl] = e_v
                            sc_free[h2][sl] = e_v
                    jp = jt - 1
                    if jp >= 0:
                        slp = jp % 2
                        for h2 in range(2):
                            PSpv = pvb[h2]
                            wait("tensor", sa, exp_done[h2][slp])
                            if jp == 0:
                                wait("tensor", sv, pvb_free[hp % 2][h2])
                            mm(PSpv[0:65, 0:512],
                               v_aug[:, jp * 520 + 65 * (2 * hp + h2):
                                     jp * 520 + 65 * (2 * hp + h2) + 65],
                               pT[h2][slp][:, :],
                               jp == 0, jp == nJB - 1, inc=True)
                            pv_read[h2][slp] = sp.v
                bcb = (PC, PD) if hp % 2 == 1 else (PE_, PF)
                for h2 in range(2):
                    PSpv = pvb[h2]
                    PSbc = bcb[h2]
                    hh = 2 * hp + h2
                    wait("vector", sp, pv_read[h2][(nJB - 1) % 2])
                    v_rc = vop("reciprocal", rec1[0:1, :], PSpv[64:65, 0:512],
                               inc=True)
                    wait("vector", sv, v_rc)
                    v_rb = vop("tensor_copy", rec1b[0:1, :], rec1[0:1, :], inc=True)
                    wait("tensor", sv, v_rb)
                    wait("tensor", sv, bc_free[(hp + 1) % 2][h2])
                    p_bc = mm(PSbc[0:64, 0:512], ones64[0:1, 0:64], rec1b[0:1, :],
                              True, True, inc=True)
                    wait("vector", sp, p_bc)
                    v_eb = vop("tensor_copy", recb[0:64, :], PSbc[0:64, 0:512],
                               inc=True)
                    bc_free[(hp + 1) % 2][h2] = v_eb
                    wait("vector", sv, v_eb)
                    v_no = vop("tensor_mul",
                               attn_oT[64 * h2:64 * h2 + 64, hp * 512:(hp + 1) * 512],
                               PSpv[0:64, 0:512], recb[0:64, :], inc=True)
                    pvb_free[hp % 2][h2] = v_no
            wait("tensor", sv, sv.v)
            if debug:
                wait("sync", sv, sv.v)
                dma("sync", s_dbg, aodbg[:, (g * nQB + qb) * 2048:(g * nQB + qb + 1) * 2048],
                    attn_oT[:, :])
            for qs in range(4):
                bx = qs % 2
                row0 = qb * 512 + qs * 128
                src_rows = x_d[row0:row0 + 128, :] if g == 0 else \
                    acc_hbm[row0:row0 + 128, :]
                vx2 = dma("sync", s_x2[bx], x2[bx][:, :], src_rows)
                for ob in range(2):
                    PSo = wo_psum[ob]
                    if ob == 0:
                        wait("tensor", sa, pe_free_a)
                        wait("tensor", sv, pe_free_v)
                    wait("tensor", sv, wo_free[ob])
                    for c in range(4):
                        mm(PSo[:, 0:512],
                           attn_oT[:, c * 512 + qs * 128: c * 512 + (qs + 1) * 128],
                           Wo_g[:, c * 1024 + ob * 512: c * 1024 + (ob + 1) * 512],
                           c == 0, (g == 1 and c == 3),
                           inc=(g == 1 and c == 3))
                    if g == 0:
                        mm(PSo[:, 0:512], ones1[0:1, :],
                           bo_bf[0:1, ob * 512:(ob + 1) * 512], False, False)
                        mm(PSo[:, 0:512],
                           gatesT_pad[:, row0:row0 + 128],
                           b2pad[:, ob * 512:(ob + 1) * 512], False, True,
                           inc=True)
                    p_wo = sp.v
                    wait("vector", sp, p_wo)
                    wait("vector", s_x2[bx], vx2)
                    wait("vector", s_ac[bx], s_ac[bx].v)
                    v_acc = vop("tensor_add", acc_st[bx][:, ob * 512:(ob + 1) * 512],
                                PSo[:, 0:512], x2[bx][:, ob * 512:(ob + 1) * 512],
                                inc=True)
                    wo_free[ob] = v_acc
                    if ob == 0:
                        pe_free_v = v_acc
                wait("sync", sv, sv.v)
                dma("sync", s_ac[bx], acc_hbm[row0:row0 + 128, :], acc_st[bx][:, :])
            qb_ld_free = sp.v

    flush_phase()
    es_attn.close()

    G = min(4, nTTH)
    NB = 4 * G
    es_p3 = ExitStack()
    xnT_h = sbuf("xnT_h", [128, 8 * (S // 2)], BF16, es_p3)
    yacc = sbuf("yacc", [128, nTTH * 1024], F32, es_p3)
    W1s = [sbuf(f"W1s{i}", [128, 8 * HID], BF16, es_p3) for i in range(2)]
    W2s = [sbuf(f"W2s{i}", [128, 8 * D], BF16, es_p3) for i in range(2)]
    b1e = [sbuf(f"b1e{i}", [1, HID], BF16, es_p3) for i in range(2)]
    hbuf = [sbuf(f"hbuf{i}", [128, HID], BF16, es_p3) for i in range(NB)]
    st6f = sbuf("st6f", [128, 12], F32, es_p3)
    mvAf = sbuf("mvAf", [128, 2], F32, es_p3)
    mvBf = sbuf("mvBf", [128, 2], F32, es_p3)
    msvsf = sbuf("msvsf", [128, 2], F32, es_p3)
    dmtf = sbuf("dmtf", [128, 2], F32, es_p3)
    ms4 = [sbuf(f"ms4_{i}", [128, G], F32, es_p3) for i in range(2)]
    vs4 = [sbuf(f"vs4_{i}", [128, G], F32, es_p3) for i in range(2)]
    dm4 = [sbuf(f"dm4_{i}", [128, G], F32, es_p3) for i in range(2)]
    var4g = [sbuf(f"var4g{i}", [128, G], F32, es_p3) for i in range(2)]
    rstd4 = [sbuf(f"rstd4_{i}", [128, G], F32, es_p3) for i in range(2)]
    nmr4 = [sbuf(f"nmr4_{i}", [128, G], F32, es_p3) for i in range(2)]

    for eng in ENGS:
        for c in (sp, sv, sa):
            wait(eng, c, c.v)
        for i in range(2):
            wait(eng, s_ac[i], s_ac[i].v)

    half_S = S // 2
    SW = S // 2
    nGRP = nTTH // G
    slots = [(hf, e, g2) for hf in range(2) for e in range(NE) for g2 in range(nGRP)]
    NSLOT = len(slots)
    SPE = nGRP

    w1_used = [0, 0]
    w2_used = [0, 0]
    w1_loaded = [0, 0]
    w2_loaded = [0, 0]
    hb_used = [0] * NB
    hpsum_free = [0, 0]
    psT_free = 0
    ob_free = [0, 0]
    norm_done = {}
    gelu_done = {}
    stats_done = {}
    hmm_done = {}
    trans_done = {}
    w2_last = {}
    xh_loaded = [0, 0]
    yi_loaded = [0, 0]
    h_last_of_half = {}

    def load_w1(j):
        if j >= 2 * NE:
            return
        hf, e = divmod(j, NE)
        slot = j % 2
        wait("gpsimd", sp, w1_used[slot])
        dma("gpsimd", s_we[slot],
            W1s[slot][:, :].rearrange("p (c m) -> p c m", c=8),
            W1_d[e].rearrange("(c p) m -> p c m", p=128))
        w1_loaded[slot] = dma("gpsimd", s_we[slot], b1e[slot][0:1, :],
                              b1_d[e:e + 1, :])

    def load_w2(j):
        if j >= 2 * NE:
            return
        hf, e = divmod(j, NE)
        slot = j % 2
        wait("gpsimd", sp, w2_used[slot])
        w2_loaded[slot] = dma("gpsimd", s_w2[slot],
                              W2s[slot][:, :].rearrange("p (c m) -> p c m", c=8),
                              W2_d[e].rearrange("(c p) m -> p c m", p=128))

    v_xh = dma("sync", s_xh, xnT_h[:, :].rearrange("p (c m) -> p c m", c=8),
               xnTh3[:, :, 0:half_S])
    xh_loaded[0] = v_xh
    yi_loaded[0] = dma("sync", s_yi,
                       yacc[:, :].rearrange("p (q d) -> p q d", q=nTTH),
                       acc3[:, 0:nTTH, :])
    w1s_of = {}
    w2s_of = {}
    load_w1(0); load_w2(0)
    w1s_of[0], w2s_of[0] = w1_loaded[0], w2_loaded[0]
    load_w1(1); load_w2(1)
    w1s_of[1], w2s_of[1] = w1_loaded[1], w2_loaded[1]

    def h_tile(k, tg):
        hf, e, g2 = slots[k]
        j = hf * NE + e
        slot = j % 2
        tt = g2 * G + tg
        bb = (k * G + tg) % 2
        bidx = (k * G + tg) % NB
        PSh = PA if bb == 0 else PB
        if tg == 0 and g2 == 0:
            wait("tensor", s_we[slot], w1s_of[j])
            if e == 0:
                wait("tensor", s_xh, xh_loaded[hf])
        wait("tensor", sv, hpsum_free[bb])
        wait("tensor", sp, hb_used[bidx])
        for hh2 in range(2):
            for c in range(8):
                mm(PSh[:, hh2 * 512:(hh2 + 1) * 512],
                   xnT_h[:, c * SW + tt * 128: c * SW + (tt + 1) * 128],
                   W1s[slot][:, c * HID + hh2 * 512: c * HID + (hh2 + 1) * 512],
                   c == 0, False)
            mm(PSh[:, hh2 * 512:(hh2 + 1) * 512], ones1[0:1, :],
               b1e[slot][0:1, hh2 * 512:(hh2 + 1) * 512], False, True,
               inc=(hh2 == 1))
        hmm_done[(k, tg)] = sp.v
        wait("vector", sp, hmm_done[(k, tg)])
        vop("bn_stats", st6f[:, 0:6], PSh[:, 0:512])
        vop("bn_stats", st6f[:, 6:12], PSh[:, 512:1024])
        vop("bn_aggr", mvAf[:, :], st6f[:, 0:6])
        v_ag = vop("bn_aggr", mvBf[:, :], st6f[:, 6:12], inc=True)
        wait("vector", sv, v_ag)
        vop("tensor_add", ms4[k % 2][:, tg:tg + 1], mvAf[:, 0:1], mvBf[:, 0:1])
        vop("tensor_add", vs4[k % 2][:, tg:tg + 1], mvAf[:, 1:2], mvBf[:, 1:2])
        vop("tensor_sub", dm4[k % 2][:, tg:tg + 1], mvAf[:, 0:1], mvBf[:, 0:1])
        hpsum_free[bb] = vop("tensor_copy", hbuf[bidx][:, :], PSh[:, 0:1024],
                             inc=True)

    def h_slot_tail(k):
        hf, e, g2 = slots[k]
        kr = k % 2
        v_ag = vop("tensor_mul", dm4[kr][:, :], dm4[kr][:, :], dm4[kr][:, :],
                   inc=True)
        wait("vector", sv, v_ag)
        stats_done[k] = vop("scalar_tensor_tensor", out=var4g[kr][:, :],
                            in0=vs4[kr][:, :], scalar=2.0, in1=dm4[kr][:, :],
                            op0=ALU.mult, op1=ALU.add, inc=True)
        if g2 == nGRP - 1 and e == NE - 1:
            h_last_of_half[hf] = sp.v
        if g2 == nGRP - 1:
            w1_used[(hf * NE + e) % 2] = sp.v

    def stage_RSTD_NORM(k):
        kr = k % 2
        wait("scalar", sv, stats_done[k])
        a_ln = act(nmr4[kr][:, :], var4g[kr][:, :], AF.Ln, scale=0.25,
                   bias=epsT[:, 0:1], inc=True)
        wait("scalar", sa, a_ln)
        a_rs = act(rstd4[kr][:, :], nmr4[kr][:, :], AF.Exp, scale=-0.5, inc=True)
        wait("vector", sa, a_rs)
        v_nm = vop("tensor_mul", nmr4[kr][:, :], ms4[kr][:, :], rstd4[kr][:, :],
                   inc=True)
        wait("vector", sv, v_nm)
        v_nm = vop("tensor_scalar", nmr4[kr][:, :], nmr4[kr][:, :], -0.5, None,
                   op0=ALU.mult, inc=True)
        wait("scalar", sv, v_nm)
        for tg in range(G):
            bidx = (k * G + tg) % NB
            act(hbuf[bidx][:, :], hbuf[bidx][:, :], AF.Identity,
                scale=rstd4[kr][:, tg:tg + 1], bias=nmr4[kr][:, tg:tg + 1],
                inc=(tg == G - 1))
        norm_done[k] = sa.v

    def tw_tile(kk, tg):
        k1 = kk - 2
        k2 = kk - 3
        if 0 <= k1 < NSLOT:
            bidx = (k1 * G + tg) % NB
            if tg == 0:
                wait("tensor", sa, norm_done[k1])
            wait("tensor", sv, psT_free_ref[0])
            for c in range(8):
                PSt = PC if c < 4 else PD
                mm(PSt[:, (c % 4) * 128:((c % 4) + 1) * 128],
                   hbuf[bidx][:, c * 128:(c + 1) * 128], I128[:, :],
                   True, True, inc=(c == 7))
            trans_done[(k1, tg)] = sp.v
            hf1, e1, g21 = slots[k1]
            wait("vector", sp, trans_done[(k1, tg)])
            for c in range(8):
                PSt = PC if c < 4 else PD
                vop("tensor_scalar", hbuf[bidx][:, c * 128:(c + 1) * 128],
                    PSt[:, (c % 4) * 128:((c % 4) + 1) * 128],
                    g1T[:, e1 * 8 + c:e1 * 8 + c + 1],
                    be1T[:, e1 * 8 + c:e1 * 8 + c + 1],
                    op0=ALU.mult, op1=ALU.add, inc=(c == 7))
            psT_free_ref[0] = sv.v
            wait("scalar", sv, sv.v)
            gelu_done[(k1, tg)] = act(hbuf[bidx][:, :], hbuf[bidx][:, :],
                                      AF.Gelu, inc=True)
        if 0 <= k2 < NSLOT:
            hf2, e2, g22 = slots[k2]
            j2 = hf2 * NE + e2
            slot2 = j2 % 2
            tt2 = g22 * G + tg
            gtt2 = hf2 * nTTH + tt2
            bidx2 = (k2 * G + tg) % NB
            wait("tensor", sa, gelu_done[(k2, tg)])
            if tg == 0 and g22 == 0:
                wait("tensor", s_w2[slot2], w2s_of[j2])
            for ob in range(2):
                PSo = PE_ if ob == 0 else PF
                wait("tensor", sv, ob_free[ob])
                for c in range(8):
                    mm(PSo[:, 0:512],
                       hbuf[bidx2][:, c * 128:(c + 1) * 128],
                       W2s[slot2][:, c * D + ob * 512: c * D + (ob + 1) * 512],
                       c == 0, c == 7, inc=(c == 7))
                p_o = sp.v
                wait("vector", sp, p_o)
                if e2 == 0 and g22 == 0 and tg == 0 and ob == 0:
                    wait("vector", s_yi, yi_loaded[hf2])
                ob_free[ob] = vop(
                    "scalar_tensor_tensor",
                    out=yacc[:, tt2 * 1024 + ob * 512: tt2 * 1024 + (ob + 1) * 512],
                    in0=PSo[:, 0:512],
                    scalar=gates_all[:, NE * gtt2 + e2:NE * gtt2 + e2 + 1],
                    in1=yacc[:, tt2 * 1024 + ob * 512: tt2 * 1024 + (ob + 1) * 512],
                    op0=ALU.mult, op1=ALU.add, inc=True)
            hb_used[bidx2] = sp.v
            if tg == G - 1:
                w2_last[j2] = sp.v
                if g22 == nGRP - 1:
                    w2_used[j2 % 2] = sp.v

    psT_free_ref = [0]

    for kk in range(NSLOT + 3):
        if 0 <= kk - 1 < NSLOT:
            stage_RSTD_NORM(kk - 1)
        if kk < NSLOT:
            for tg in range(G):
                h_tile(kk, tg)
                if tg >= 1:
                    tw_tile(kk, tg - 1)
            h_slot_tail(kk)
            tw_tile(kk, G - 1)
            hf, e, g2 = slots[kk]
            j = hf * NE + e
            if g2 == nGRP - 1:
                if j + 2 < 2 * NE:
                    load_w1(j + 2)
                    w1s_of[j + 2] = w1_loaded[(j + 2) % 2]
                if j == NE - 1:
                    wait("sync", sp, h_last_of_half[0])
                    xh_loaded[1] = dma(
                        "sync", s_xh,
                        xnT_h[:, :].rearrange("p (c m) -> p c m", c=8),
                        xnTh3[:, :, half_S:2 * half_S])
        else:
            for tg in range(G):
                tw_tile(kk, tg)
        if 0 <= kk - 3 < NSLOT:
            hf2, e2, g22 = slots[kk - 3]
            j2 = hf2 * NE + e2
            if g22 == nGRP - 1:
                if j2 + 2 < 2 * NE:
                    load_w2(j2 + 2)
                    w2s_of[j2 + 2] = w2_loaded[(j2 + 2) % 2]
                if e2 == NE - 1:
                    wait("sync", sv, sv.v)
                    dma("sync", s_out,
                        out3[:, hf2 * nTTH:(hf2 + 1) * nTTH, :],
                        yacc[:, :].rearrange("p (q d) -> p q d", q=nTTH))
                    if hf2 == 0:
                        wait("sync", s_out, 16)
                        yi_loaded[1] = dma(
                            "sync", s_yi,
                            yacc[:, :].rearrange("p (q d) -> p q d", q=nTTH),
                            acc3[:, nTTH:2 * nTTH, :])

    wait("sync", s_out, s_out.v)
    flush_phase()
    es_p3.close()
    es_top.close()
    return nc


_NC_CACHE = {}


def kernel(**inp):
    S = 4096
    ncore = 8
    if S not in _NC_CACHE:
        _NC_CACHE[S] = build_nc(S)
    nc = _NC_CACHE[S]

    f32 = np.float32
    x = np.ascontiguousarray(inp["L0_attn_x"], dtype=f32)
    shared = {
        "Wq": np.ascontiguousarray(inp["L0_attn_Wq"], f32),
        "Wl": np.ascontiguousarray(inp["L0_attn_Wl"], f32),
        "Wk": np.ascontiguousarray(inp["L0_attn_Wk"], f32),
        "Wv": np.ascontiguousarray(inp["L0_attn_Wv"], f32),
        "Wo": np.ascontiguousarray(inp["L0_attn_Wo"], f32),
        "Wr": np.ascontiguousarray(inp["L0_ffn_Wr"], f32),
        "bqT": np.ascontiguousarray(inp["L0_attn_bq"].reshape(8, 128).T, f32),
        "bkT": np.ascontiguousarray(inp["L0_attn_bk"].reshape(8, 128).T, f32),
        "blT": np.ascontiguousarray(inp["L0_attn_bl"].reshape(2, 128).T, f32),
        "bv": np.ascontiguousarray(inp["L0_attn_bv"].reshape(1, D), f32),
        "bo": np.ascontiguousarray(inp["L0_attn_bo"].reshape(1, D), f32),
        "br": np.ascontiguousarray(inp["L0_ffn_br"].reshape(1, NE), f32),
        "W1": np.ascontiguousarray(inp["L0_ffn_W1"], f32),
        "b1": np.ascontiguousarray(inp["L0_ffn_b1"], f32),
        "g1T": np.ascontiguousarray(
            inp["L0_ffn_g1"].reshape(NE, 8, 128).transpose(2, 0, 1).reshape(128, 128), f32),
        "be1T": np.ascontiguousarray(
            inp["L0_ffn_be1"].reshape(NE, 8, 128).transpose(2, 0, 1).reshape(128, 128), f32),
        "W2": np.ascontiguousarray(inp["L0_ffn_W2"], f32),
        "b2": np.ascontiguousarray(inp["L0_ffn_b2"], f32),
        "eye": np.eye(128, dtype=f32),
    }
    in_maps = []
    for c in range(ncore):
        m = dict(shared)
        m["x"] = np.ascontiguousarray(x[c])
        in_maps.append(m)
    res = run_bass_kernel_spmd(nc, in_maps, core_ids=list(range(ncore)))
    out = np.stack([np.asarray(r["out"], dtype=np.float32) for r in res.results],
                   axis=0)
    return out
